# version 17
# baseline (speedup 1.0000x reference)
import numpy as np
from contextlib import ExitStack

import concourse.bass as bass
import concourse.mybir as mybir
from concourse.ap import AP
from concourse.bass_utils import run_bass_kernel_spmd

F32 = mybir.dt.float32
BF16 = mybir.dt.bfloat16
I32 = mybir.dt.int32
AF = mybir.ActivationFunctionType
ALU = mybir.AluOpType
AX = mybir.AxisListType

T = 2048
D = 1024
STATE = 2048
FFN0 = 3072
FFN1 = 4096
NH = 16
EXPC = 20.0
SCW = 2560

DEBUG_STOP = None
DEBUG_N = 16
DEBUG_PART = 9
DEBUG_DUMP = None



_EMIT = {
    "tensor_copy", "tensor_tensor", "tensor_add", "tensor_sub", "tensor_mul",
    "tensor_max", "tensor_scalar", "tensor_single_scalar", "tensor_scalar_add",
    "tensor_scalar_sub", "tensor_scalar_mul", "tensor_scalar_max",
    "tensor_scalar_min", "tensor_relu", "tensor_reduce", "reduce_max",
    "reduce_sum", "scalar_tensor_tensor", "tensor_tensor_scan", "reciprocal",
    "memset", "memzero", "iota", "affine_select", "activation", "copy",
    "select", "transpose", "bn_stats", "bn_aggr", "engine_nop",
}

_OUT0 = {
    "tensor_add", "tensor_sub", "tensor_mul", "tensor_max", "tensor_scalar_add",
    "tensor_scalar_sub", "tensor_scalar_mul", "tensor_scalar_max",
    "tensor_scalar_min", "reciprocal", "memset", "memzero", "tensor_relu",
}


def _ap_name(x):
    try:
        return x.name
    except Exception:
        return None


class _W:

    def __init__(self, eng, sem):
        self._e = eng
        self._m = sem

    def __getattr__(self, name):
        fn = getattr(self._e, name)
        if name not in _EMIT:
            return fn

        def wrap(*a, **k):
            reads, writes = set(), set()
            for i, x in enumerate(a):
                n = _ap_name(x)
                if n is None:
                    continue
                (writes if (i == 0 and name in _OUT0) else reads).add(n)
            for kk, x in k.items():
                n = _ap_name(x)
                if n is None:
                    continue
                (writes if kk in ("out", "accum_out", "out_ap") else reads).add(n)
            pend = self._m.pend
            if (reads & pend) or (writes & pend) or (writes & self._m.rd):
                self._e.wait_ge(self._m.h, self._m.n)
                pend.clear()
                self._m.rd.clear()
            ins = fn(*a, **k)
            ins.then_inc(self._m.h, 1)
            self._m.n += 1
            self._m.last = ins
            pend.update(writes)
            self._m.rd.update(reads)
            return ins
        return wrap

class _WT:

    def __init__(self, eng, sem):
        self._e = eng
        self._m = sem

    def __getattr__(self, name):
        fn = getattr(self._e, name)
        if name not in ("matmul", "transpose"):
            return fn

        def wrap(*a, **k):
            ins = fn(*a, **k)
            ins.then_inc(self._m.h, 1)
            self._m.n += 1
            self._m.last = ins
            return ins
        return wrap


class _WBlock:
    def __init__(self, b, sems):
        self._b = b
        self._sems = sems

    def vector(self, f):
        return self._b.vector(lambda e, __f=f: __f(_W(e, self._sems["v"])))

    def scalar(self, f):
        return self._b.scalar(lambda e, __f=f: __f(_W(e, self._sems["s"])))

    def gpsimd(self, f):
        return self._b.gpsimd(lambda e, __f=f: __f(_W(e, self._sems["g"])))

    def tensor(self, f):
        return self._b.tensor(lambda e, __f=f: __f(_WT(e, self._sems["t"])))

    def sync(self, f):
        return self._b.sync(f)


class Sem:
    def __init__(self, h):
        self.h = h
        self.n = 0
        self.last = None
        self.pend = set()
        self.rd = set()

    def inc(self, instr, k=1):
        if instr is self.last:
            return
        instr.then_inc(self.h, k)
        self.n += k


def build(nc: bass.Bass):
    dx = nc.dram_tensor("x", [T, D], F32, kind="ExternalInput").ap()
    w = {}
    for name, shape in [
        ("L0_ls", [512]), ("L0_rs", [512]), ("L0_ss", [D]),
        ("L0_Wk", [D, STATE]), ("L0_Wugg", [D, 3 * STATE]), ("L0_Wout", [STATE, D]),
        ("L0_wg", [FFN0, D]), ("L0_wu", [FFN0, D]), ("L0_wd", [D, FFN0]),
        ("L1_Wq", [D, D]), ("L1_Wk", [D, D]), ("L1_Wv", [D, D]), ("L1_Wo", [D, D]),
        ("L1_bo", [D]), ("L1_sc", [NH, 128, SCW]),
        ("L1_wg", [FFN1, D]), ("L1_wu", [FFN1, D]), ("L1_wd", [D, FFN1]),
    ]:
        w[name] = nc.dram_tensor(name, shape, F32, kind="ExternalInput").ap()
    dout = nc.dram_tensor("out", [T, D], F32, kind="ExternalOutput").ap()

    d_xn = nc.dram_tensor("d_xn", [16, 8, 128, 128], F32).ap()
    d_ssm = nc.dram_tensor("d_ssm", [16, 128, T], BF16).ap()
    d_up = nc.dram_tensor("d_up", [24, 128, T], F32).ap()
    d_gate = nc.dram_tensor("d_gate", [24, 128, T], F32).ap()
    d_g2 = nc.dram_tensor("d_g2", [24, 128, T], F32).ap()
    d_o = nc.dram_tensor("d_o", [24, 128, T], F32).ap()
    d_g = nc.dram_tensor("d_g", [8, 128, T], F32).ap()
    d_gate1 = nc.dram_tensor("d_gate1", [32, 128, T], F32).ap()
    d_up1 = nc.dram_tensor("d_up1", [32, 128, T], F32).ap()
    d_g1 = nc.dram_tensor("d_g1", [8, 128, T], F32).ap()
    d_wdT1 = nc.dram_tensor("d_wdT1", [8, 128, FFN1], BF16).ap()
    cc_in = nc.dram_tensor("cc_in", [1, 8], F32)
    cc_out = nc.dram_tensor("cc_out", [1, 8], F32)

    st = ExitStack()
    sbt = lambda name, shape, dt=F32: st.enter_context(nc.sbuf_tensor(name, shape, dt))

    with st:
        xnbf = sbt("xnbf", [128, 8 * T], BF16)
        xqbf = sbt("xqbf", [128, 8 * T], BF16)
        x1 = sbt("x1", [128, 8 * T])
        SCR = 10240
        scr = sbt("scr", [128, SCR])
        scrb = scr[:].bitcast(BF16)
        ident = sbt("ident", [128, 128])
        identb = sbt("identb", [128, 128], BF16)
        ones_r = sbt("ones_r", [1, 128])
        ones_c = sbt("ones_c", [128, 1])
        fs0 = sbt("fs0", [128, 8])
        bo_sb = sbt("bo_sb", [128, 8])
        stats = sbt("stats", [128, 64])
        blks = sbt("blks", [128, 288])
        rowa = sbt("rowa", [1, 1024])
        rowb = sbt("rowb", [1, 1024])
        rowq = sbt("rowq", [1, 2048])
        rowc = sbt("rowc", [1, 64])
        cc_sb = sbt("cc_sb", [1, 8])
        stage = sbt("stage", [128, 1024])
        stage2 = sbt("stage2", [128, 1024])
        stage3 = sbt("stage3", [128, 512])
        stage4 = stage3
        xnt = sbt("xnt", [128, 1024])
        sq_sb = sbt("sq_sb", [128, 512])
        intt = sbt("intt", [128, 1024], I32)
        pb = [st.enter_context(nc.psum_tensor(f"pb{i}", [128, 512], F32)) for i in range(8)]

        sD = Sem(st.enter_context(nc.semaphore("sD")))
        sG = Sem(st.enter_context(nc.semaphore("sG")))
        sGc = Sem(st.enter_context(nc.semaphore("sGc")))
        sT = Sem(st.enter_context(nc.semaphore("sT")))
        sV = Sem(st.enter_context(nc.semaphore("sV")))
        sS = Sem(st.enter_context(nc.semaphore("sS")))
        sSt = Sem(st.enter_context(nc.semaphore("sSt")))
        sCC = Sem(st.enter_context(nc.semaphore("sCC")))
        sDp = [Sem(st.enter_context(nc.semaphore(f"sDp{i}"))) for i in range(2)]
        sGp = [Sem(st.enter_context(nc.semaphore(f"sGp{i}"))) for i in range(2)]
        sSp = [Sem(st.enter_context(nc.semaphore(f"sSp{i}"))) for i in range(2)]

        def pdma(sy, par, out, in_):
            sy.dma_start(out=out, in_=in_).then_inc(sDp[par].h, 16)
            sDp[par].n += 16
            return sDp[par].n

        def pgdma(g, par, out, in_):
            g.dma_start(out=out, in_=in_).then_inc(sGp[par].h, 16)
            sGp[par].n += 16
            return sGp[par].n

        def pstdma(sy, par, out, in_):
            sy.dma_start(out=out, in_=in_).then_inc(sSp[par].h, 16)
            sSp[par].n += 16
            return sSp[par].n

        def drain_parity(eng):
            for s in (sDp[0], sDp[1], sGp[0], sGp[1], sSp[0], sSp[1]):
                if s.n > 0:
                    eng.wait_ge(s.h, s.n)

        block = _WBlock(st.enter_context(nc.Block()), {"v": sV, "s": sS, "g": sGc, "t": sT})

        ALL = (sD, sG, sGc, sT, sV, sS, sSt)

        def waits(eng, *sems):
            for s in sems:
                if s.n > 0:
                    eng.wait_ge(s.h, s.n)

        def recip_newton(v, src, dst, tmp):
            v.reciprocal(dst, src)
            v.tensor_mul(tmp, src, dst)
            v.tensor_scalar(out=tmp, in0=tmp, scalar1=-1.0, scalar2=2.0, op0=ALU.mult, op1=ALU.add)
            v.tensor_mul(dst, dst, tmp)

        def scale_cols(v, src, scol, rcol, qmax):
            v.tensor_scalar(out=stats[:, scol:scol + 1], in0=src, scalar1=1.0 / qmax,
                            scalar2=1e-12, op0=ALU.mult, op1=ALU.max)
            recip_newton(v, stats[:, scol:scol + 1], stats[:, rcol:rcol + 1], stats[:, 19:20])

        def part_absmax(col, cccol):
            @block.tensor
            def _(t):
                waits(t, sV, sS)
                t.transpose(pb[0][0:1, 0:128], stats[:, col:col + 1], ident[:])
                sT.inc(t.nop())

            @block.vector
            def _(v):
                waits(v, sT)
                sV.inc(v.tensor_reduce(out=cc_sb[0:1, cccol:cccol + 1], in_=pb[0][0:1, 0:128],
                                       axis=AX.X, op=ALU.max, apply_absolute_value=True))

        def allreduce_max():
            @block.gpsimd
            def _(g):
                waits(g, sV)
                g.dma_start(out=cc_in.ap(), in_=cc_sb[:]).then_inc(sG.h, 16)
                sG.n += 16
                waits(g, sG)
                sCC.inc(g.collective_compute(
                    "AllReduce", ALU.max, replica_groups=[list(range(8))],
                    ins=[cc_in.ap().opt()], outs=[cc_out.ap().opt()]))
                waits(g, sCC)
                g.dma_start(out=cc_sb[:], in_=cc_out.ap()).then_inc(sG.h, 16)
                sG.n += 16

        def bcast_cc(col, bank):
            @block.tensor
            def _(t):
                waits(t, sG, sV, sS)
                t.matmul(pb[bank][:, 0:1], ones_r[:], cc_sb[0:1, col:col + 1], start=True, stop=True)
                sT.inc(t.nop())

        def gdma(g, out, in_):
            g.dma_start(out=out, in_=in_).then_inc(sG.h, 16)
            sG.n += 16

        def sdma(sy, out, in_, **kw):
            sy.dma_start(out=out, in_=in_, **kw).then_inc(sD.h, 16)
            sD.n += 16

        def stdma(sy, out, in_):
            sy.dma_start(out=out, in_=in_).then_inc(sSt.h, 16)
            sSt.n += 16

        def emit_dummy_output():
            @block.gpsimd
            def _(g):
                waits(g, sT, sV, sS, sD, sG, sSt)
                sGc.inc(g.memset(stage[:], 0.0))

            @block.sync
            def _(sy):
                waits(sy, sGc, sSt)
                for tt in range(16):
                    stdma(sy, dout[tt * 128:(tt + 1) * 128, :], stage[:])
                waits(sy, sSt)

        @block.gpsimd
        def _(g):
            sGc.inc(g.memset(ident[:], 0.0))
            sGc.inc(g.memset(identb[:], 0.0))
            g.wait_ge(sGc.h, sGc.n)
            g.affine_select(out=ident[:], in_=ident[:], compare_op=ALU.not_equal,
                            fill=1.0, base=0, pattern=[[-1, 128]], channel_multiplier=1)
            g.affine_select(out=identb[:], in_=identb[:], compare_op=ALU.not_equal,
                            fill=1.0, base=0, pattern=[[-1, 128]], channel_multiplier=1)
            g.memset(ones_r[:], 1.0)
            g.memset(ones_c[:], 1.0)
            g.memset(cc_sb[:], 0.0)
            sGc.inc(g.memset(stats[:], 0.0))

        @block.sync
        def _(sy):
            sdma(sy, stage[:, 0:4], AP(w["L0_ls"].tensor, 0, [[1, 128], [128, 4]]), allow_slow_non_contiguous=True)
            sdma(sy, stage[:, 4:8], AP(w["L0_rs"].tensor, 0, [[1, 128], [128, 4]]), allow_slow_non_contiguous=True)
            sdma(sy, stage[:, 8:16], AP(w["L0_ss"].tensor, 0, [[1, 128], [128, 8]]), allow_slow_non_contiguous=True)
            sdma(sy, bo_sb[:], AP(w["L1_bo"].tensor, 0, [[1, 128], [128, 8]]), allow_slow_non_contiguous=True)

        @block.vector
        def _(v):
            waits(v, sD, sGc)
            v.tensor_mul(fs0[:, 0:4], stage[:, 0:4], stage[:, 8:12])
            sV.inc(v.tensor_mul(fs0[:, 4:8], stage[:, 4:8], stage[:, 12:16]))

        ld_m = {}
        for tt in range(16):
            xt = [stage, stage2][tt % 2]

            @block.sync
            def _(sy, tt=tt, xt=xt):
                waits(sy, sV, sS, sSt)
                ld_m[tt] = pdma(sy, tt % 2, xt[:], dx[tt * 128:(tt + 1) * 128, :])

            @block.vector
            def _(v, xt=xt, tt=tt):
                v.wait_ge(sDp[tt % 2].h, ld_m[tt])
                waits(v, sS)
                v.reduce_sum(stats[:, 8:9], xt[:], axis=AX.X)
                v.tensor_scalar_mul(stats[:, 9:10], stats[:, 8:9], 1.0 / D)
                v.tensor_scalar_mul(stats[:, 10:11], stats[:, 9:10], -1.0)
                v.tensor_mul(stats[:, 11:12], stats[:, 9:10], stats[:, 9:10])
                sV.inc(v.tensor_scalar(out=stats[:, 12:13], in0=stats[:, 11:12],
                                       scalar1=-1.0, scalar2=1e-5, op0=ALU.mult, op1=ALU.add))

            @block.scalar
            def _(s, xt=xt, tt=tt):
                s.wait_ge(sDp[tt % 2].h, ld_m[tt])
                waits(s, sV)
                s.activation(out=sq_sb[:], in_=xt[:, 0:512], func=AF.Square, accum_out=stats[:, 13:14])
                s.activation(out=sq_sb[:], in_=xt[:, 512:1024], func=AF.Square, accum_out=stats[:, 14:15])
                sS.inc(s.nop())

            @block.vector
            def _(v):
                waits(v, sS)
                sV.inc(v.tensor_add(stats[:, 15:16], stats[:, 13:14], stats[:, 14:15]))

            @block.scalar
            def _(s):
                waits(s, sV)
                sS.inc(s.activation(out=stats[:, 24:25], in_=stats[:, 15:16], func=AF.Sqrt,
                                    scale=1.0 / D, bias=stats[:, 12:13]))

            @block.vector
            def _(v, xt=xt):
                waits(v, sS, sT)
                recip_newton(v, stats[:, 24:25], stats[:, 16:17], stats[:, 19:20])
                v.tensor_scalar(out=xnt[:], in0=xt[:], scalar1=stats[:, 10:11],
                                scalar2=stats[:, 16:17], op0=ALU.add, op1=ALU.mult)
                v.tensor_reduce(out=stats[:, 17:18], in_=xnt[:], axis=AX.X,
                                op=ALU.max, apply_absolute_value=True)
                sV.inc(v.tensor_max(stats[:, 0:1], stats[:, 0:1], stats[:, 17:18]))

            @block.scalar
            def _(s):
                waits(s, sV)
                s.activation(out=sq_sb[:], in_=xnt[:, 0:512], func=AF.Square, accum_out=stats[:, 13:14])
                s.activation(out=sq_sb[:], in_=xnt[:, 512:1024], func=AF.Square, accum_out=stats[:, 14:15])
                s.activation(out=stats[:, 20:21], in_=stats[:, 13:14], func=AF.Sqrt, scale=1.0 / 512.0)
                sS.inc(s.activation(out=stats[:, 21:22], in_=stats[:, 14:15], func=AF.Sqrt, scale=1.0 / 512.0))

            @block.vector
            def _(v, xt=xt):
                waits(v, sS)
                v.tensor_scalar_add(stats[:, 20:21], stats[:, 20:21], 1e-8)
                v.tensor_scalar_add(stats[:, 21:22], stats[:, 21:22], 1e-8)
                recip_newton(v, stats[:, 20:21], stats[:, 22:23], stats[:, 19:20])
                recip_newton(v, stats[:, 21:22], stats[:, 23:24], stats[:, 19:20])
                v.tensor_scalar(out=xt[:, 0:512], in0=xnt[:, 0:512],
                                scalar1=stats[:, 22:23], scalar2=None, op0=ALU.mult)
                sV.inc(v.tensor_scalar(out=xt[:, 512:1024], in0=xnt[:, 512:1024],
                                       scalar1=stats[:, 23:24], scalar2=None, op0=ALU.mult))

            @block.tensor
            def _(t, xt=xt):
                waits(t, sV, sS)
                for c in range(8):
                    t.transpose(pb[4 + c // 4][:, (c % 4) * 128:(c % 4) * 128 + 128],
                                xnt[:, c * 128:(c + 1) * 128], ident[:])
                for c in range(8):
                    t.transpose(pb[6 + c // 4][:, (c % 4) * 128:(c % 4) * 128 + 128],
                                xt[:, c * 128:(c + 1) * 128], ident[:])
                sT.inc(t.nop())

            @block.scalar
            def _(s, xt=xt):
                waits(s, sT, sSt)
                for c in range(8):
                    s.copy(out=xt[:, c * 128:(c + 1) * 128],
                           in_=pb[4 + c // 4][:, (c % 4) * 128:(c % 4) * 128 + 128])
                sS.inc(s.nop())

            @block.vector
            def _(v, tt=tt):
                waits(v, sT)
                for c in range(8):
                    v.tensor_scalar(out=xnbf[:, c * T + tt * 128: c * T + (tt + 1) * 128],
                                    in0=pb[6 + c // 4][:, (c % 4) * 128:(c % 4) * 128 + 128],
                                    scalar1=fs0[:, c:c + 1], scalar2=None, op0=ALU.mult)
                sV.inc(v.nop())

            @block.sync
            def _(sy, tt=tt, xt=xt):
                waits(sy, sS)
                stdma(sy, AP(d_xn.tensor, tt * 8 * 128 * 128, [[128, 128], [128 * 128, 8], [1, 128]]),
                      xt[:].rearrange("p (c f) -> p c f", c=8))

        if DEBUG_STOP == "p1":
            emit_dummy_output()
            return nc

        part_absmax(0, 0)
        allreduce_max()

        if DEBUG_STOP == "mx":
            emit_dummy_output()
            return nc

        KS, U2, GO, HH = 0, 2048, 4096, 6144

        def wt0(par, q):
            base = 8192 * 2 + q * 1024
            return scrb[:, base:base + 1024]

        wugg_cols = [None, 0, STATE, 2 * STATE]
        stgb0 = stage[:].bitcast(BF16)
        stgb1 = stage2[:].bitcast(BF16)

        def wt0(par, q):
            if par == 0:
                return scrb[:, 8192 * 2 + q * 1024:8192 * 2 + (q + 1) * 1024]
            if q < 2:
                return stgb0[:, q * 1024:(q + 1) * 1024]
            return stgb1[:, (q - 2) * 1024:(q - 1) * 1024]

        p2_ld, p2_mm, p2_ev, p2_vc, p2_scan, p2_st = {}, {}, {}, {}, {}, {}
        for sc in range(DEBUG_N):
            par = sc % 2

            @block.gpsimd
            def _(g, sc=sc, par=par):
                if sc == 0:
                    waits(g, sT, sS, sV, sSt)
                    drain_parity(g)
                if (sc - 2, 3) in p2_mm:
                    g.wait_ge(sT.h, p2_mm[(sc - 2, 3)])
                pgdma(g, par, wt0(par, 0).rearrange("p (c f) -> p c f", c=8),
                      AP(w["L0_Wk"].tensor, sc * 128, [[STATE, 128], [128 * STATE, 8], [1, 128]]))
                for q in (1, 2, 3):
                    pgdma(g, par, wt0(par, q).rearrange("p (c f) -> p c f", c=8),
                          AP(w["L0_Wugg"].tensor, wugg_cols[q] + sc * 128,
                             [[3 * STATE, 128], [128 * 3 * STATE, 8], [1, 128]]))
                p2_ld[sc] = sGp[par].n

            for tb in range(4):
                @block.tensor
                def _(t, sc=sc, tb=tb, par=par):
                    t.wait_ge(sGp[par].h, p2_ld[sc])
                    if sc == 0 and tb == 0:
                        waits(t, sV, sS)
                    key = sc * 4 + tb - 2
                    if key >= 0:
                        t.wait_ge(sS.h, p2_ev[key][0])
                        t.wait_ge(sV.h, p2_ev[key][1])
                    bs = 4 * (tb % 2)
                    for q in range(4):
                        for c in range(8):
                            t.matmul(pb[bs + q][:],
                                     wt0(par, q)[:, c * 128:(c + 1) * 128],
                                     xnbf[:, c * T + tb * 512: c * T + tb * 512 + 512],
                                     start=(c == 0), stop=(c == 7))
                    p2_mm[(sc, tb)] = sT.n

                @block.scalar
                def _(s, sc=sc, tb=tb):
                    s.wait_ge(sT.h, p2_mm[(sc, tb)])
                    if sc >= 1:
                        s.wait_ge(sV.h, p2_scan[sc - 1])
                    if tb <= 1 and sc >= 2:
                        s.wait_ge(sSt.h, p2_st[sc - 1])
                    bs = 4 * (tb % 2)
                    s.activation(out=scr[:, KS + tb * 512: KS + tb * 512 + 512], in_=pb[bs][:], func=AF.Sigmoid)
                    s.activation(out=scr[:, HH + tb * 512: HH + tb * 512 + 512], in_=pb[bs + 2][:], func=AF.Sigmoid)
                    s.activation(out=scr[:, GO + tb * 512: GO + tb * 512 + 512], in_=pb[bs + 3][:], func=AF.Sigmoid)
                    p2_ev[sc * 4 + tb] = [sS.n, None]

                @block.vector
                def _(v, sc=sc, tb=tb):
                    v.wait_ge(sS.h, p2_ev[sc * 4 + tb][0])
                    if tb <= 1 and sc >= 1:
                        v.wait_ge(sSt.h, p2_st.get(sc - 1, 0) or 0) if False else None
                    if tb <= 1 and sc >= 1 and (sc - 1) in p2_st:
                        v.wait_ge(sSt.h, p2_st[sc - 1])
                    bs = 4 * (tb % 2)
                    sl = slice(U2 + tb * 512, U2 + tb * 512 + 512)
                    hs = slice(HH + tb * 512, HH + tb * 512 + 512)
                    ks = slice(KS + tb * 512, KS + tb * 512 + 512)
                    v.tensor_mul(scr[:, sl], pb[bs + 1][:], scr[:, hs])
                    v.tensor_mul(scr[:, hs], scr[:, sl], scr[:, ks])
                    v.tensor_sub(scr[:, sl], scr[:, sl], scr[:, hs])
                    p2_ev[sc * 4 + tb][1] = sV.n

            @block.vector
            def _(v, sc=sc):
                v.tensor_copy(out=scr[:, HH:HH + 1], in_=scr[:, U2:U2 + 1])
                v.tensor_tensor_scan(out=scr[:, HH + 1:HH + T], data0=scr[:, KS:KS + T - 1],
                                     data1=scr[:, U2 + 1:U2 + T], initial=scr[:, HH:HH + 1],
                                     op0=ALU.mult, op1=ALU.add)
                v.tensor_mul(scrb[:, U2 * 2:U2 * 2 + T], scr[:, HH:HH + T], scr[:, GO:GO + T])
                p2_scan[sc] = sV.n

            @block.sync
            def _(sy, sc=sc):
                sy.wait_ge(sV.h, p2_scan[sc])
                stdma(sy, d_ssm[sc], scrb[:, U2 * 2:U2 * 2 + T])
                p2_st[sc] = sSt.n

        if DEBUG_STOP == "p2":
            emit_dummy_output()
            return nc

        bcast_cc(0, 0)

        @block.vector
        def _(v):
            waits(v, sT)
            scale_cols(v, pb[0][:, 0:1], 2, 3, 32767.0)
            sV.inc(v.nop())

        xq_ld = {}
        for c in range(8):
            @block.sync
            def _(sy, c=c):
                waits(sy, sV, sS, sSt)
                if c == 0:
                    drain_parity(sy)
                xq_ld[c] = pdma(sy, c % 2, scr[:, (c % 2) * 2048:(c % 2) * 2048 + 2048].rearrange("p (a f) -> p a f", a=16),
                                AP(d_xn.tensor, c * 128 * 128, [[128, 128], [8 * 128 * 128, 16], [1, 128]]))

            @block.scalar
            def _(s, c=c):
                s.wait_ge(sDp[c % 2].h, xq_ld[c])
                waits(s, sV)
                for hh in range(2):
                    s.activation(out=intt[:], in_=scr[:, (c % 2) * 2048 + hh * 1024:(c % 2) * 2048 + (hh + 1) * 1024],
                                 func=AF.Copy, scale=stats[:, 3:4])
                    s.activation(out=xqbf[:, c * T + hh * 1024:c * T + (hh + 1) * 1024], in_=intt[:],
                                 func=AF.Copy, scale=stats[:, 2:3])
                sS.inc(s.nop())

        if DEBUG_STOP == "p3":
            emit_dummy_output()
            return nc


        def transpose_evac(nblocks, src_bf16_off, dst_slice_fn):
            for g0 in range(0, nblocks, 8):
                gn = min(8, nblocks - g0)

                @block.tensor
                def _(t, g0=g0, gn=gn):
                    waits(t, sV, sS)
                    for j in range(gn):
                        ic = g0 + j
                        t.transpose(pb[6 + j // 4][:, (j % 4) * 64:(j % 4) * 64 + 64].bitcast(BF16),
                                    scrb[:, src_bf16_off + ic * 128:src_bf16_off + (ic + 1) * 128],
                                    identb[:])
                    sT.inc(t.nop())

                @block.scalar
                def _(s, g0=g0, gn=gn):
                    waits(s, sT)
                    for j in range(gn):
                        s.copy(out=dst_slice_fn(g0 + j),
                               in_=pb[6 + j // 4][:, (j % 4) * 64:(j % 4) * 64 + 64].bitcast(BF16))
                    sS.inc(s.nop())

        p4q, p4tr, p4ev, p4ld = {}, {}, {}, {}
        p4mm, p4mmev, p4st, p4mm4 = {}, {}, {}, {}
        for idx in range(48):
            wname, ddst, mcol = [("L0_wu", d_up, 5), ("L0_wg", d_gate, 6)][idx // 24]
            oc = idx % 24
            WR = (idx % 2) * 1024
            WQT = 3584 + (idx % 2) * 512

            @block.sync
            def _(sy, wname=wname, oc=oc, WR=WR, idx=idx):
                if idx == 0:
                    waits(sy, sV, sS, sT)
                    drain_parity(sy)
                if idx - 2 in p4q:
                    sy.wait_ge(sV.h, p4q[idx - 2])
                p4ld[idx] = pdma(sy, idx % 2, scr[:, WR:WR + 1024], w[wname][oc * 128:(oc + 1) * 128, :])

            @block.scalar
            def _(s, WR=WR, idx=idx):
                s.wait_ge(sDp[idx % 2].h, p4ld[idx])
                if idx - 1 in p4q:
                    s.wait_ge(sV.h, p4q[idx - 1])
                s.activation(out=scr[:, 2048:3072], in_=scr[:, WR:WR + 1024], func=AF.Abs)
                sabs = sS.n

            sabs = sS.n

            @block.vector
            def _(v, WR=WR, idx=idx):
                v.wait_ge(sS.h, sabs)
                if idx - 1 in p4tr:
                    v.wait_ge(sT.h, p4tr[idx - 1])
                nblk = 32
                v.tensor_reduce(out=blks[:, 0:nblk].rearrange("p (b o) -> p b o", o=1),
                                in_=scr[:, 2048:3072].rearrange("p (b k) -> p b k", k=32),
                                axis=AX.X, op=ALU.max)
                v.tensor_scalar(out=blks[:, 96:96 + nblk], in0=blks[:, 0:nblk],
                                scalar1=1.0 / 7.0, scalar2=1e-12, op0=ALU.mult, op1=ALU.max)
                recip_newton(v, blks[:, 96:96 + nblk], blks[:, 192:192 + nblk], blks[:, 0:nblk])
                v.tensor_tensor(out=intt[:, 0:1024].rearrange("p (b k) -> p b k", k=32),
                                in0=scr[:, WR:WR + 1024].rearrange("p (b k) -> p b k", k=32),
                                in1=blks[:, 192:192 + nblk].rearrange("p (b o) -> p b o", o=1)
                                .to_broadcast([128, nblk, 32]), op=ALU.mult)
                v.tensor_tensor(out=scrb[:, 3072 * 2:3072 * 2 + 1024].rearrange("p (b k) -> p b k", k=32),
                                in0=intt[:, 0:1024].rearrange("p (b k) -> p b k", k=32),
                                in1=blks[:, 96:96 + nblk].rearrange("p (b o) -> p b o", o=1)
                                .to_broadcast([128, nblk, 32]), op=ALU.mult)
                p4q[idx] = sV.n

            @block.tensor
            def _(t, idx=idx, WQT=WQT):
                t.wait_ge(sV.h, p4q[idx])
                if idx - 1 in p4ev:
                    t.wait_ge(sS.h, p4ev[idx - 1])
                for ic in range(8):
                    t.transpose(pb[6 + ic // 4][:, (ic % 4) * 64:(ic % 4) * 64 + 64].bitcast(BF16),
                                scrb[:, 3072 * 2 + ic * 128:3072 * 2 + (ic + 1) * 128], identb[:])
                p4tr[idx] = sT.n

            @block.scalar
            def _(s, idx=idx, WQT=WQT):
                s.wait_ge(sT.h, p4tr[idx])
                if idx - 2 in p4mm4:
                    s.wait_ge(sT.h, p4mm4[idx - 2])
                for ic in range(8):
                    s.copy(out=scrb[:, WQT * 2 + ic * 128:WQT * 2 + (ic + 1) * 128],
                           in_=pb[6 + ic // 4][:, (ic % 4) * 64:(ic % 4) * 64 + 64].bitcast(BF16))
                p4ev[idx] = sS.n

            for tb in range(4):
                @block.tensor
                def _(t, tb=tb, idx=idx, WQT=WQT):
                    t.wait_ge(sS.h, p4ev[idx])
                    if idx == 0 and tb == 0:
                        waits(t, sS, sV)
                    key = (idx, tb)
                    if (idx - 1, tb) in p4mmev:
                        t.wait_ge(sS.h, p4mmev[(idx - 1, tb)][0])
                        t.wait_ge(sV.h, p4mmev[(idx - 1, tb)][1])
                    for ic in range(8):
                        t.matmul(pb[tb][:], scrb[:, WQT * 2 + ic * 128:WQT * 2 + (ic + 1) * 128],
                                 xqbf[:, ic * T + tb * 512: ic * T + tb * 512 + 512],
                                 start=(ic == 0), stop=(ic == 7))
                    p4mm[(idx, tb)] = sT.n

                @block.scalar
                def _(s, tb=tb, idx=idx):
                    s.wait_ge(sT.h, p4mm[(idx, tb)])
                    prev = (idx, tb - 2) if tb >= 2 else (idx - 1, tb + 2)
                    if prev in p4st:
                        s.wait_ge(sSp[tb % 2].h, p4st[prev])
                    s.copy(out=[stage3, sq_sb][tb % 2][:], in_=pb[tb][:])

                @block.vector
                def _(v, tb=tb, idx=idx, mcol=mcol):
                    v.wait_ge(sT.h, p4mm[(idx, tb)])
                    v.tensor_reduce(out=stats[:, 18:19], in_=pb[tb][:], axis=AX.X,
                                    op=ALU.max, apply_absolute_value=True)
                    v.tensor_max(stats[:, mcol:mcol + 1], stats[:, mcol:mcol + 1], stats[:, 18:19])
                p4mmev[(idx, tb)] = (sS.n, sV.n)

                @block.sync
                def _(sy, ddst=ddst, oc=oc, tb=tb, idx=idx):
                    sy.wait_ge(sS.h, p4mmev[(idx, tb)][0])
                    p4st[(idx, tb)] = pstdma(sy, tb % 2, ddst[oc, :, tb * 512:(tb + 1) * 512],
                                             [stage3, sq_sb][tb % 2][:])
            p4mm4[idx] = p4mm[(idx, 3)]

        if DEBUG_STOP == "p4":
            emit_dummy_output()
            return nc

        part_absmax(5, 1)
        part_absmax(6, 2)
        allreduce_max()
        bcast_cc(1, 0)
        bcast_cc(2, 1)

        @block.vector
        def _(v):
            waits(v, sT)
            scale_cols(v, pb[0][:, 0:1], 20, 21, 32767.0)
            scale_cols(v, pb[1][:, 0:1], 22, 23, 32767.0)
            sV.inc(v.tensor_scalar_mul(stats[:, 24:25], stats[:, 22:23], 1.0 / 65536.0))

        if DEBUG_STOP == "cc2":
            emit_dummy_output()
            return nc

        stage_i = stage[:].bitcast(I32)
        a_ld, a_v = {}, {}
        for oc in range(24):
            @block.sync
            def _(sy, oc=oc):
                if oc == 0:
                    waits(sy, sV, sS)
                    drain_parity(sy)
                if oc - 2 in a_v:
                    sy.wait_ge(sV.h, a_v[oc - 2])
                a_ld[oc] = pdma(sy, oc % 2, [scr[:, 0:2048], scr[:, 6144:8192]][oc % 2], d_gate[oc])

            for hh in range(2):
                H = hh * 1024
                GOF = 0 if oc % 2 == 0 else 6144

                @block.vector
                def _(v, H=H, GOF=GOF, oc=oc):
                    v.wait_ge(sDp[oc % 2].h, a_ld[oc])
                    waits(v, sS, sSt)
                    v.tensor_scalar(out=intt[:], in0=scr[:, GOF + H:GOF + H + 1024],
                                    scalar1=stats[:, 23:24], scalar2=None, op0=ALU.mult)
                    sV.inc(v.nop())

                @block.scalar
                def _(s, H=H):
                    waits(s, sV)
                    sS.inc(s.activation(out=scr[:, 2048 + H:2048 + H + 1024], in_=intt[:],
                                        func=AF.Sigmoid, scale=stats[:, 22:23]))

                @block.vector
                def _(v, H=H):
                    waits(v, sS)
                    v.tensor_scalar_mul(stage_i, scr[:, 2048 + H:2048 + H + 1024], 65536.0)
                    v.tensor_scalar_min(stage_i, stage_i, 65535.0)
                    v.tensor_mul(scr[:, 4096 + H:4096 + H + 1024], intt[:], stage_i)
                    v.tensor_scalar(out=scr[:, 4096 + H:4096 + H + 1024],
                                    in0=scr[:, 4096 + H:4096 + H + 1024],
                                    scalar1=stats[:, 24:25], scalar2=None, op0=ALU.mult)
                    v.tensor_reduce(out=stats[:, 18:19], in_=scr[:, 4096 + H:4096 + H + 1024],
                                    axis=AX.X, op=ALU.max, apply_absolute_value=True)
                    sV.inc(v.tensor_max(stats[:, 7:8], stats[:, 7:8], stats[:, 18:19]))

            @block.sync
            def _(sy, oc=oc):
                waits(sy, sV)
                stdma(sy, d_g2[oc], scr[:, 4096:6144])
                a_v[oc] = sV.n

        part_absmax(7, 3)
        allreduce_max()
        bcast_cc(3, 0)

        @block.vector
        def _(v):
            waits(v, sT)
            scale_cols(v, pb[0][:, 0:1], 26, 27, 32767.0)
            sV.inc(v.tensor_mul(stats[:, 28:29], stats[:, 26:27], stats[:, 20:21]))

        if DEBUG_STOP == "p5a":
            emit_dummy_output()
            return nc

        b5_ld, b5_v = {}, {}
        for oc in range(24):
            @block.sync
            def _(sy, oc=oc):
                if oc == 0:
                    waits(sy, sV)
                    drain_parity(sy)
                if oc - 2 in b5_v:
                    sy.wait_ge(sV.h, b5_v[oc - 2])
                IN = [0, 6144][oc % 2]
                pdma(sy, oc % 2, scr[:, IN:IN + 2048], d_g2[oc])
                b5_ld[oc] = pdma(sy, oc % 2, scr[:, 2048:4096] if oc % 2 == 0 else scr[:, 8192:10240], d_up[oc])

            for hh in range(2):
                H = hh * 1024

                @block.vector
                def _(v, H=H, oc=oc):
                    v.wait_ge(sDp[oc % 2].h, b5_ld[oc])
                    waits(v, sSt)
                    GOF = [0, 6144][oc % 2]
                    UOF = [2048, 8192][oc % 2]
                    v.tensor_scalar(out=intt[:], in0=scr[:, GOF + H:GOF + H + 1024],
                                    scalar1=stats[:, 27:28], scalar2=None, op0=ALU.mult)
                    v.tensor_scalar(out=stage_i, in0=scr[:, UOF + H:UOF + H + 1024],
                                    scalar1=stats[:, 21:22], scalar2=None, op0=ALU.mult)
                    v.tensor_mul(scr[:, 4096 + H:4096 + H + 1024], intt[:], stage_i)
                    v.tensor_scalar(out=scr[:, 4096 + H:4096 + H + 1024],
                                    in0=scr[:, 4096 + H:4096 + H + 1024],
                                    scalar1=stats[:, 28:29], scalar2=None, op0=ALU.mult)
                    v.tensor_reduce(out=stats[:, 18:19], in_=scr[:, 4096 + H:4096 + H + 1024],
                                    axis=AX.X, op=ALU.max, apply_absolute_value=True)
                    sV.inc(v.tensor_max(stats[:, 25:26], stats[:, 25:26], stats[:, 18:19]))

            @block.sync
            def _(sy, oc=oc):
                waits(sy, sV)
                stdma(sy, d_o[oc], scr[:, 4096:6144])
                b5_v[oc] = sV.n

        part_absmax(25, 4)
        allreduce_max()
        bcast_cc(4, 0)

        @block.vector
        def _(v):
            waits(v, sT)
            scale_cols(v, pb[0][:, 0:1], 29, 30, 32767.0)
            sV.inc(v.nop())

        if DEBUG_STOP == "p5b":
            emit_dummy_output()
            return nc

        def wdT_slice(dm, ic):
            if dm < 5:
                return xnbf[:, dm * 3072 + ic * 128: dm * 3072 + (ic + 1) * 128]
            return xqbf[:, (dm - 5) * 3072 + ic * 128: (dm - 5) * 3072 + (ic + 1) * 128]

        for dm in range(8):
            @block.sync
            def _(sy, dm=dm):
                waits(sy, sV, sS, sT)
                sdma(sy, scr[:, 0:3072], w["L0_wd"][dm * 128:(dm + 1) * 128, :])

            @block.scalar
            def _(s):
                waits(s, sD)
                sS.inc(s.activation(out=scr[:, 3072:6144], in_=scr[:, 0:3072], func=AF.Abs))

            @block.vector
            def _(v):
                waits(v, sS, sT)
                nblk = 96
                v.tensor_reduce(out=blks[:, 0:nblk].rearrange("p (b o) -> p b o", o=1),
                                in_=scr[:, 3072:6144].rearrange("p (b k) -> p b k", k=32),
                                axis=AX.X, op=ALU.max)
                v.tensor_scalar(out=blks[:, 96:96 + nblk], in0=blks[:, 0:nblk],
                                scalar1=1.0 / 7.0, scalar2=1e-12, op0=ALU.mult, op1=ALU.max)
                recip_newton(v, blks[:, 96:96 + nblk], blks[:, 192:192 + nblk], blks[:, 0:nblk])
                for half in range(4):
                    h0 = half * 768
                    b0 = half * 24
                    v.tensor_tensor(out=intt[:, 0:768].rearrange("p (b k) -> p b k", k=32),
                                    in0=scr[:, h0:h0 + 768].rearrange("p (b k) -> p b k", k=32),
                                    in1=blks[:, 192 + b0:192 + b0 + 24].rearrange("p (b o) -> p b o", o=1)
                                    .to_broadcast([128, 24, 32]), op=ALU.mult)
                    v.tensor_tensor(out=scrb[:, 6144 * 2 + h0:6144 * 2 + h0 + 768].rearrange("p (b k) -> p b k", k=32),
                                    in0=intt[:, 0:768].rearrange("p (b k) -> p b k", k=32),
                                    in1=blks[:, 96 + b0:96 + b0 + 24].rearrange("p (b o) -> p b o", o=1)
                                    .to_broadcast([128, 24, 32]), op=ALU.mult)
                sV.inc(v.nop())

            transpose_evac(24, 6144 * 2, lambda ic, dm=dm: wdT_slice(dm, ic))

        if DEBUG_STOP == "p5c":
            emit_dummy_output()
            return nc

        for tb in range(4):
            for oc in range(24):
                i = tb * 24 + oc
                SRC = (i % 2) * 512
                IO = (i % 2) * 512

                @block.sync
                def _(sy, oc=oc, tb=tb, SRC=SRC):
                    waits(sy, sS, sV, sT)
                    sdma(sy, scr[:, SRC:SRC + 512], d_o[oc, :, tb * 512:(tb + 1) * 512])

                @block.vector
                def _(v, oc=oc, SRC=SRC, IO=IO):
                    waits(v, sD, sS)
                    v.tensor_scalar(out=intt[:, IO:IO + 512], in0=scr[:, SRC:SRC + 512],
                                    scalar1=stats[:, 30:31], scalar2=None, op0=ALU.mult)
                    sV.inc(v.tensor_scalar(out=scrb[:, 4096 * 2 + oc * 512:4096 * 2 + oc * 512 + 512],
                                           in0=intt[:, IO:IO + 512],
                                           scalar1=stats[:, 29:30], scalar2=None, op0=ALU.mult))

            for dm in range(8):
                @block.tensor
                def _(t, dm=dm):
                    waits(t, sS, sV)
                    for ic in range(24):
                        t.matmul(pb[dm % 4][:], wdT_slice(dm, ic),
                                 scrb[:, 4096 * 2 + ic * 512:4096 * 2 + ic * 512 + 512],
                                 start=(ic == 0), stop=(ic == 23))
                    sT.inc(t.nop())

                @block.scalar
                def _(s, dm=dm):
                    waits(s, sT)
                    sS.inc(s.activation(out=sq_sb[:], in_=pb[dm % 4][:], func=AF.Sigmoid))

                @block.vector
                def _(v, dm=dm):
                    waits(v, sS, sSt)
                    sV.inc(v.tensor_mul([stage3, stage4][dm % 2][:], pb[dm % 4][:], sq_sb[:]))

                @block.sync
                def _(sy, tb=tb, dm=dm):
                    waits(sy, sV)
                    stdma(sy, d_g[dm, :, tb * 512:(tb + 1) * 512], [stage3, stage4][dm % 2][:])

        if DEBUG_STOP == "p5d":
            emit_dummy_output()
            return nc

        for tb in range(4):
            @block.sync
            def _(sy, tb=tb):
                waits(sy, sS, sV, sT)
                sdma(sy, scrb[:, 0:16 * 512].rearrange("p (a f) -> p a f", a=16),
                     AP(d_ssm.tensor, tb * 512, [[T, 128], [128 * T, 16], [1, 512]]))
                for i in range(4):
                    sdma(sy, scr[:, 4096 + i * 1024:4096 + (i + 1) * 1024],
                         dx[tb * 512 + i * 128: tb * 512 + (i + 1) * 128, :])

            for dm in range(8):
                @block.gpsimd
                def _(g, dm=dm):
                    waits(g, sT)
                    gdma(g, scrb[:, 9216 * 2:9216 * 2 + 2048].rearrange("p (c f) -> p c f", c=16),
                         AP(w["L0_Wout"].tensor, dm * 128, [[D, 128], [128 * D, 16], [1, 128]]))

                @block.sync
                def _(sy, tb=tb, dm=dm):
                    waits(sy, sV)
                    sdma(sy, scr[:, 8192 + (dm % 2) * 512:8192 + (dm % 2) * 512 + 512],
                         d_g[dm, :, tb * 512:(tb + 1) * 512])
                    sdma(sy, stage3[:].rearrange("p (a f) -> p a f", a=4),
                         AP(d_xn.tensor, dm * 128 * 128 + (tb * 4) * 8 * 128 * 128,
                            [[128, 128], [8 * 128 * 128, 4], [1, 128]]))

                @block.tensor
                def _(t, tb=tb, dm=dm):
                    waits(t, sG, sD, sV)
                    bk = dm % 2
                    wbase = 9216 * 2
                    for sc2 in range(16):
                        t.matmul(pb[bk][:], scrb[:, wbase + sc2 * 128:wbase + (sc2 + 1) * 128],
                                 scrb[:, sc2 * 512:(sc2 + 1) * 512],
                                 start=(sc2 == 0), stop=(sc2 == 15))
                    for i in range(4):
                        t.transpose(pb[4 + bk][:, i * 128:(i + 1) * 128],
                                    scr[:, 4096 + i * 1024 + dm * 128:4096 + i * 1024 + (dm + 1) * 128],
                                    ident[:])
                    sT.inc(t.nop())

                @block.vector
                def _(v, tb=tb, dm=dm):
                    waits(v, sT, sD)
                    bk = dm % 2
                    d0 = dm * T + tb * 512
                    v.tensor_add(x1[:, d0:d0 + 512], pb[bk][:], stage3[:])
                    v.tensor_mul(x1[:, d0:d0 + 512], x1[:, d0:d0 + 512],
                                 scr[:, 8192 + bk * 512:8192 + bk * 512 + 512])
                    sV.inc(v.tensor_add(x1[:, d0:d0 + 512], x1[:, d0:d0 + 512], pb[4 + bk][:]))

        def emit_output(src):
            for tb in range(4):
                for dm in range(8):
                    @block.tensor
                    def _(t, tb=tb, dm=dm):
                        waits(t, sV, sS)
                        for i in range(4):
                            t.transpose(pb[6][:, i * 128:(i + 1) * 128],
                                        src[:, dm * T + tb * 512 + i * 128: dm * T + tb * 512 + (i + 1) * 128],
                                        ident[:])
                        sT.inc(t.nop())

                    @block.scalar
                    def _(s, tb=tb, dm=dm):
                        waits(s, sT, sSt)
                        for i in range(4):
                            s.copy(out=scr[:, i * 1024 + dm * 128: i * 1024 + (dm + 1) * 128],
                                   in_=pb[6][:, i * 128:(i + 1) * 128])
                        sS.inc(s.nop())

                @block.sync
                def _(sy, tb=tb):
                    waits(sy, sS)
                    for i in range(4):
                        stdma(sy, dout[tb * 512 + i * 128: tb * 512 + (i + 1) * 128, :],
                              scr[:, i * 1024:(i + 1) * 1024])

        if DEBUG_STOP == "l0":
            emit_output(x1)

            @block.sync
            def _(sy):
                waits(sy, sSt)
            return nc

        if DEBUG_STOP == "p6pre":
            emit_dummy_output()
            return nc

        for tb in range(4):
            for c in range(8):
                @block.scalar
                def _(s, c=c, tb=tb):
                    waits(s, sV, sT)
                    sS.inc(s.activation(out=sq_sb[:], in_=x1[:, c * T + tb * 512:c * T + tb * 512 + 512],
                                        func=AF.Square))

                @block.tensor
                def _(t, c=c, tb=tb):
                    waits(t, sS)
                    t.matmul(pb[1][0:1, :], ones_c[:], sq_sb[:], start=(c == 0), stop=(c == 7))
                    t.matmul(pb[0][0:1, :], ones_c[:], x1[:, c * T + tb * 512:c * T + tb * 512 + 512],
                             start=(c == 0), stop=(c == 7))
                    sT.inc(t.nop())

            @block.vector
            def _(v):
                waits(v, sT)
                v.tensor_scalar_mul(rowb[0:1, 0:512], pb[0][0:1, :], 1.0 / D)
                v.tensor_mul(rowa[0:1, 0:512], rowb[0:1, 0:512], rowb[0:1, 0:512])
                v.tensor_scalar_mul(rowa[0:1, 0:512], rowa[0:1, 0:512], -1.0)
                v.scalar_tensor_tensor(out=rowa[0:1, 0:512], in0=pb[1][0:1, :], scalar=1.0 / D,
                                       in1=rowa[0:1, 0:512], op0=ALU.mult, op1=ALU.add)
                sV.inc(v.tensor_scalar_add(rowa[0:1, 0:512], rowa[0:1, 0:512], 1e-5))

            @block.scalar
            def _(s):
                waits(s, sV)
                sS.inc(s.activation(out=rowa[0:1, 0:512], in_=rowa[0:1, 0:512], func=AF.Sqrt))

            @block.vector
            def _(v, tb=tb):
                waits(v, sS)
                recip_newton(v, rowa[0:1, 0:512], rowa[0:1, 512:1024],
                             rowq[0:1, tb * 512:tb * 512 + 512])
                sV.inc(v.tensor_copy(out=rowa[0:1, 0:512], in_=rowa[0:1, 512:1024]))

            @block.vector
            def _(v):
                waits(v, sS)
                v.tensor_mul(rowb[0:1, 0:512], rowb[0:1, 0:512], rowa[0:1, 0:512])
                sV.inc(v.tensor_scalar_mul(rowb[0:1, 0:512], rowb[0:1, 0:512], -1.0))

            @block.tensor
            def _(t):
                waits(t, sV)
                t.matmul(pb[2][:], ones_r[:], rowa[0:1, 0:512], start=True, stop=True)
                t.matmul(pb[3][:], ones_r[:], rowb[0:1, 0:512], start=True, stop=True)
                sT.inc(t.nop())

            @block.vector
            def _(v, tb=tb):
                waits(v, sT)
                for c in range(8):
                    v.scalar_tensor_tensor(out=scr[:, c * 512:(c + 1) * 512],
                                           in0=x1[:, c * T + tb * 512:c * T + tb * 512 + 512],
                                           scalar=1.0, in1=pb[2][:], op0=ALU.mult, op1=ALU.mult)
                    v.tensor_add(scr[:, c * 512:(c + 1) * 512], scr[:, c * 512:(c + 1) * 512], pb[3][:])
                    v.tensor_copy(out=xnbf[:, c * T + tb * 512:c * T + tb * 512 + 512],
                                  in_=scr[:, c * 512:(c + 1) * 512])
                v.scalar_tensor_tensor(out=scr[:, 4096:4608], in0=scr[:, 0:512], scalar=-1.0,
                                       in1=scr[:, 0:512], op0=ALU.mult, op1=ALU.max)
                for c in range(1, 8):
                    v.scalar_tensor_tensor(out=scr[:, 4608:5120], in0=scr[:, c * 512:(c + 1) * 512],
                                           scalar=-1.0, in1=scr[:, c * 512:(c + 1) * 512],
                                           op0=ALU.mult, op1=ALU.max)
                    v.tensor_max(scr[:, 4096:4608], scr[:, 4096:4608], scr[:, 4608:5120])
                sV.inc(v.nop())

            @block.tensor
            def _(t):
                waits(t, sV)
                for i in range(4):
                    t.transpose(pb[4][:, i * 128:(i + 1) * 128],
                                scr[:, 4096 + i * 128:4096 + (i + 1) * 128], ident[:])
                sT.inc(t.nop())

            @block.vector
            def _(v):
                waits(v, sT)
                for i in range(4):
                    v.tensor_reduce(out=stats[:, 32 + i:33 + i], in_=pb[4][:, i * 128:(i + 1) * 128],
                                    axis=AX.X, op=ALU.max)
                sV.inc(v.nop())

            @block.tensor
            def _(t):
                waits(t, sV)
                for i in range(4):
                    t.transpose(pb[5][0:1, i * 128:(i + 1) * 128], stats[:, 32 + i:33 + i], ident[:])
                sT.inc(t.nop())

            @block.vector
            def _(v, tb=tb):
                waits(v, sT)
                v.tensor_scalar_max(rowa[0:1, 0:512], pb[5][0:1, 0:512], 1e-5)
                v.tensor_scalar_mul(rowq[0:1, tb * 512:(tb + 1) * 512], rowa[0:1, 0:512], 1.0 / 127.0)
                recip_newton(v, rowa[0:1, 0:512], rowb[0:1, 0:512], rowa[0:1, 512:1024])
                sV.inc(v.tensor_scalar_mul(rowb[0:1, 0:512], rowb[0:1, 0:512], 127.0))

            @block.tensor
            def _(t):
                waits(t, sV)
                t.matmul(pb[6][:], ones_r[:], rowb[0:1, 0:512], start=True, stop=True)
                sT.inc(t.nop())

            for c in range(8):
                @block.vector
                def _(v, c=c):
                    waits(v, sT, sS)
                    sV.inc(v.tensor_tensor(out=intt[:, (c % 2) * 512:(c % 2) * 512 + 512],
                                           in0=scr[:, c * 512:(c + 1) * 512], in1=pb[6][:], op=ALU.mult))

                @block.scalar
                def _(s, c=c, tb=tb):
                    waits(s, sV)
                    sS.inc(s.copy(out=xqbf[:, c * T + tb * 512:c * T + tb * 512 + 512],
                                  in_=intt[:, (c % 2) * 512:(c % 2) * 512 + 512]))

        if DEBUG_STOP == "p7":
            emit_dummy_output()
            return nc

        for wi, (wname, chunks, nI) in enumerate([("L1_wg", 32, 1024), ("L1_wu", 32, 1024), ("L1_wd", 8, 4096)]):
            acc = 46 + wi
            for oc in range(chunks):
                @block.sync
                def _(sy, wname=wname, oc=oc, nI=nI):
                    waits(sy, sS, sV)
                    sdma(sy, scr[:, 0:nI], w[wname][oc * 128:(oc + 1) * 128, :])

                @block.scalar
                def _(s, nI=nI):
                    waits(s, sD, sV)
                    sS.inc(s.activation(out=scr[:, 4096:4096 + nI], in_=scr[:, 0:nI], func=AF.Abs))

                @block.vector
                def _(v, nI=nI, acc=acc, oc=oc):
                    waits(v, sS)
                    v.tensor_reduce(out=stats[:, 36:37], in_=scr[:, 4096:4096 + nI], axis=AX.X, op=ALU.add)
                    if oc == 0:
                        sV.inc(v.tensor_copy(out=stats[:, acc:acc + 1], in_=stats[:, 36:37]))
                    else:
                        sV.inc(v.tensor_add(stats[:, acc:acc + 1], stats[:, acc:acc + 1], stats[:, 36:37]))

            @block.tensor
            def _(t, acc=acc):
                waits(t, sV)
                t.transpose(pb[0][0:1, 0:128], stats[:, acc:acc + 1], ident[:])
                sT.inc(t.nop())

            @block.vector
            def _(v, wi=wi, chunks=chunks, nI=nI):
                waits(v, sT)
                v.tensor_reduce(out=rowc[0:1, wi:wi + 1], in_=pb[0][0:1, 0:128], axis=AX.X, op=ALU.add)
                sV.inc(v.tensor_scalar(out=rowc[0:1, 8 + wi:9 + wi], in0=rowc[0:1, wi:wi + 1],
                                       scalar1=1.0 / (chunks * 128 * nI), scalar2=1e-5,
                                       op0=ALU.mult, op1=ALU.max))

        @block.tensor
        def _(t):
            waits(t, sV)
            for wi in range(3):
                t.matmul(pb[wi][:, 0:1], ones_r[:], rowc[0:1, 8 + wi:9 + wi], start=True, stop=True)
            sT.inc(t.nop())

        @block.vector
        def _(v):
            waits(v, sT)
            for wi in range(3):
                v.tensor_copy(out=stats[:, 41 + 2 * wi:42 + 2 * wi], in_=pb[wi][:, 0:1])
                recip_newton(v, stats[:, 41 + 2 * wi:42 + 2 * wi], stats[:, 40 + 2 * wi:41 + 2 * wi],
                             stats[:, 19:20])
            sV.inc(v.nop())

        b_ld, b_q, b_tn, b_tr, b_te, b_mm, b_ev, b_st, b_mm4 = {}, {}, {}, {}, {}, {}, {}, {}, {}
        for wi, (wname, ddst) in enumerate([("L1_wg", d_gate1), ("L1_wu", d_up1)]):
            scol = 40 + 2 * wi
            invcol = 41 + 2 * wi
            for tb in range(4):
                @block.vector
                def _(v, invcol=invcol, tb=tb):
                    waits(v, sS, sT)
                    sV.inc(v.tensor_scalar(out=rowa[0:1, 0:512], in0=rowq[0:1, tb * 512:(tb + 1) * 512],
                                           scalar1=stats[0:1, invcol:invcol + 1], scalar2=None, op0=ALU.mult))

                @block.tensor
                def _(t, tb=tb):
                    waits(t, sV, sS)
                    t.matmul(pb[6][:], ones_r[:], rowa[0:1, 0:512], start=True, stop=True)
                    sT.inc(t.nop())

                @block.vector
                def _(v, tb=tb):
                    waits(v, sT)
                    sV.inc(v.tensor_copy(out=scr[:, 4096 + tb * 512:4096 + (tb + 1) * 512], in_=pb[6][:]))

            for oc in range(32):
                idx = wi * 32 + oc
                WR = (idx % 2) * 1024
                TN = 2048 + (idx % 2) * 512
                WQT = 3072 + (idx % 2) * 512

                @block.sync
                def _(sy, wname=wname, oc=oc, WR=WR, idx=idx):
                    if idx == 0:
                        waits(sy, sV, sS, sT)
                        drain_parity(sy)
                    if idx - 2 in b_q:
                        sy.wait_ge(sV.h, b_q[idx - 2])
                    b_ld[idx] = pdma(sy, idx % 2, scr[:, WR:WR + 1024], w[wname][oc * 128:(oc + 1) * 128, :])

                @block.vector
                def _(v, scol=scol, WR=WR, idx=idx):
                    v.wait_ge(sDp[idx % 2].h, b_ld[idx])
                    if idx - 1 in b_tn:
                        v.wait_ge(sS.h, b_tn[idx - 1])
                    v.tensor_scalar(out=intt[:, 0:1024], in0=scr[:, WR:WR + 1024],
                                    scalar1=stats[:, scol:scol + 1], scalar2=1.0, op0=ALU.mult, op1=ALU.min)
                    v.tensor_scalar_max(intt[:, 0:1024], intt[:, 0:1024], -1.0)
                    b_q[idx] = sV.n

                @block.scalar
                def _(s, TN=TN, idx=idx):
                    s.wait_ge(sV.h, b_q[idx])
                    if idx - 2 in b_tr:
                        s.wait_ge(sT.h, b_tr[idx - 2])
                    s.copy(out=scrb[:, TN * 2:TN * 2 + 1024], in_=intt[:, 0:1024])
                    b_tn[idx] = sS.n

                @block.tensor
                def _(t, TN=TN, idx=idx):
                    t.wait_ge(sS.h, b_tn[idx])
                    if idx - 1 in b_te:
                        t.wait_ge(sS.h, b_te[idx - 1])
                    for ic in range(8):
                        t.transpose(pb[6 + ic // 4][:, (ic % 4) * 64:(ic % 4) * 64 + 64].bitcast(BF16),
                                    scrb[:, TN * 2 + ic * 128:TN * 2 + (ic + 1) * 128], identb[:])
                    b_tr[idx] = sT.n

                @block.scalar
                def _(s, WQT=WQT, idx=idx):
                    s.wait_ge(sT.h, b_tr[idx])
                    if idx - 2 in b_mm4:
                        s.wait_ge(sT.h, b_mm4[idx - 2])
                    for ic in range(8):
                        s.copy(out=scrb[:, WQT * 2 + ic * 128:WQT * 2 + (ic + 1) * 128],
                               in_=pb[6 + ic // 4][:, (ic % 4) * 64:(ic % 4) * 64 + 64].bitcast(BF16))
                    b_te[idx] = sS.n

                for tb in range(4):
                    @block.tensor
                    def _(t, tb=tb, idx=idx, WQT=WQT):
                        t.wait_ge(sS.h, b_te[idx])
                        if (idx - 1, tb) in b_ev:
                            t.wait_ge(sV.h, b_ev[(idx - 1, tb)])
                        for ic in range(8):
                            t.matmul(pb[tb % 2][:], scrb[:, WQT * 2 + ic * 128:WQT * 2 + (ic + 1) * 128],
                                     xqbf[:, ic * T + tb * 512: ic * T + tb * 512 + 512],
                                     start=(ic == 0), stop=(ic == 7))
                        b_mm[(idx, tb)] = sT.n

                    @block.vector
                    def _(v, tb=tb, idx=idx):
                        v.wait_ge(sT.h, b_mm[(idx, tb)])
                        prev = (idx, tb - 2) if tb >= 2 else (idx - 1, tb + 2)
                        if prev in b_st:
                            v.wait_ge(sSp[tb % 2].h, b_st[prev])
                        v.tensor_mul([stage3, sq_sb][tb % 2][:], pb[tb % 2][:],
                                     scr[:, 4096 + tb * 512:4096 + (tb + 1) * 512])
                        b_ev[(idx, tb)] = sV.n

                    @block.sync
                    def _(sy, ddst=ddst, oc=oc, tb=tb, idx=idx):
                        sy.wait_ge(sV.h, b_ev[(idx, tb)])
                        b_st[(idx, tb)] = pstdma(sy, tb % 2, ddst[oc, :, tb * 512:(tb + 1) * 512],
                                                 [stage3, sq_sb][tb % 2][:])
                b_mm4[idx] = b_mm[(idx, 3)]

        for dm in range(8):
            @block.sync
            def _(sy, dm=dm):
                waits(sy, sS, sV, sT, sSt)
                sdma(sy, scr[:, 0:4096], w["L1_wd"][dm * 128:(dm + 1) * 128, :])

            @block.vector
            def _(v):
                waits(v, sD, sS)
                for half in range(4):
                    h0 = half * 1024
                    v.tensor_scalar(out=intt[:, 0:1024], in0=scr[:, h0:h0 + 1024],
                                    scalar1=stats[:, 44:45], scalar2=1.0, op0=ALU.mult, op1=ALU.min)
                    v.tensor_scalar_max(intt[:, 0:1024], intt[:, 0:1024], -1.0)
                    v.tensor_copy(out=scrb[:, 8192 * 2 + h0:8192 * 2 + h0 + 1024], in_=intt[:, 0:1024])
                sV.inc(v.nop())

            for half in range(2):
                transpose_evac(16, 8192 * 2 + half * 2048,
                               lambda ic: scrb[:, 6144 * 2 + ic * 128:6144 * 2 + (ic + 1) * 128])

                @block.sync
                def _(sy, dm=dm, half=half):
                    waits(sy, sS)
                    stdma(sy, d_wdT1[dm, :, half * 2048:(half + 1) * 2048],
                          scrb[:, 6144 * 2:6144 * 2 + 2048])

        stageb_i = stage
        for tb in range(4):
            dmk, smk, vmk = {}, {}, {}
            for c0 in range(0, 32, 2):
                @block.sync
                def _(sy, c0=c0, tb=tb):
                    if c0 == 0:
                        waits(sy, sV, sS)
                        drain_parity(sy)
                    for i in (c0, c0 + 1):
                        if i - 2 in vmk:
                            sy.wait_ge(sV.h, vmk[i - 2])
                        pdma(sy, i % 2, scr[:, (i % 2) * 512:(i % 2) * 512 + 512],
                             d_gate1[i, :, tb * 512:(tb + 1) * 512])
                        dmk[i] = pdma(sy, i % 2, scr[:, 1024 + (i % 2) * 512:1024 + (i % 2) * 512 + 512],
                                      d_up1[i, :, tb * 512:(tb + 1) * 512])

                @block.scalar
                def _(s, c0=c0):
                    for i in (c0, c0 + 1):
                        s.wait_ge(sDp[i % 2].h, dmk[i])
                        if i - 2 in vmk:
                            s.wait_ge(sV.h, vmk[i - 2])
                        s.activation(out=stage[:, (i % 2) * 512:(i % 2) * 512 + 512],
                                     in_=scr[:, (i % 2) * 512:(i % 2) * 512 + 512], func=AF.Sigmoid)
                        smk[i] = sS.n

                @block.vector
                def _(v, c0=c0):
                    for i in (c0, c0 + 1):
                        v.wait_ge(sS.h, smk[i])
                        v.tensor_mul(stage3[:], scr[:, (i % 2) * 512:(i % 2) * 512 + 512],
                                     stage[:, (i % 2) * 512:(i % 2) * 512 + 512])
                        v.tensor_mul(stage3[:], stage3[:],
                                     scr[:, 1024 + (i % 2) * 512:1024 + (i % 2) * 512 + 512])
                        v.scalar_tensor_tensor(out=stage3[:], in0=stage3[:], scalar=-1.0,
                                               in1=stage3[:], op0=ALU.mult, op1=ALU.max)
                        if i == 0:
                            v.tensor_copy(out=scr[:, 2048:2560], in_=stage3[:])
                        else:
                            v.tensor_max(scr[:, 2048:2560], scr[:, 2048:2560], stage3[:])
                        vmk[i] = sV.n

            @block.tensor
            def _(t):
                waits(t, sV)
                for i in range(4):
                    t.transpose(pb[4][:, i * 128:(i + 1) * 128],
                                scr[:, 2048 + i * 128:2048 + (i + 1) * 128], ident[:])
                sT.inc(t.nop())

            @block.vector
            def _(v):
                waits(v, sT)
                for i in range(4):
                    v.tensor_reduce(out=stats[:, 32 + i:33 + i], in_=pb[4][:, i * 128:(i + 1) * 128],
                                    axis=AX.X, op=ALU.max)
                sV.inc(v.nop())

            @block.tensor
            def _(t):
                waits(t, sV)
                for i in range(4):
                    t.transpose(pb[5][0:1, i * 128:(i + 1) * 128], stats[:, 32 + i:33 + i], ident[:])
                sT.inc(t.nop())

            @block.vector
            def _(v):
                waits(v, sT)
                v.tensor_scalar_max(rowa[0:1, 0:512], pb[5][0:1, 0:512], 1e-5)
                v.tensor_scalar(out=rowa[0:1, 512:1024], in0=rowa[0:1, 0:512],
                                scalar1=1.0 / 127.0, scalar2=None, op0=ALU.mult)
                recip_newton(v, rowa[0:1, 0:512], rowb[0:1, 0:512], rowb[0:1, 512:1024])
                v.tensor_scalar_mul(rowb[0:1, 0:512], rowb[0:1, 0:512], 127.0)
                sV.inc(v.tensor_scalar(out=rowa[0:1, 512:1024], in0=rowa[0:1, 512:1024],
                                       scalar1=stats[0:1, 45:46], scalar2=None, op0=ALU.mult))

            @block.tensor
            def _(t):
                waits(t, sV)
                t.matmul(pb[6][:], ones_r[:], rowb[0:1, 0:512], start=True, stop=True)
                t.matmul(pb[7][:], ones_r[:], rowa[0:1, 512:1024], start=True, stop=True)
                sT.inc(t.nop())

            dmk, smk, vmk = {}, {}, {}
            for c0 in range(0, 32, 2):
                @block.sync
                def _(sy, c0=c0, tb=tb):
                    if c0 == 0:
                        waits(sy, sV, sS, sT)
                        drain_parity(sy)
                    for i in (c0, c0 + 1):
                        if i - 2 in vmk:
                            sy.wait_ge(sV.h, vmk[i - 2])
                        pdma(sy, i % 2, scr[:, (i % 2) * 512:(i % 2) * 512 + 512],
                             d_gate1[i, :, tb * 512:(tb + 1) * 512])
                        dmk[i] = pdma(sy, i % 2, scr[:, 1024 + (i % 2) * 512:1024 + (i % 2) * 512 + 512],
                                      d_up1[i, :, tb * 512:(tb + 1) * 512])

                @block.scalar
                def _(s, c0=c0):
                    for i in (c0, c0 + 1):
                        s.wait_ge(sDp[i % 2].h, dmk[i])
                        if i - 2 in vmk:
                            s.wait_ge(sV.h, vmk[i - 2])
                        s.activation(out=stage[:, (i % 2) * 512:(i % 2) * 512 + 512],
                                     in_=scr[:, (i % 2) * 512:(i % 2) * 512 + 512], func=AF.Sigmoid)
                        smk[i] = sS.n

                @block.vector
                def _(v, c0=c0):
                    for i in (c0, c0 + 1):
                        v.wait_ge(sS.h, smk[i])
                        v.tensor_mul(stage3[:], scr[:, (i % 2) * 512:(i % 2) * 512 + 512],
                                     stage[:, (i % 2) * 512:(i % 2) * 512 + 512])
                        v.tensor_mul(stage3[:], stage3[:],
                                     scr[:, 1024 + (i % 2) * 512:1024 + (i % 2) * 512 + 512])
                        v.tensor_tensor(out=intt[:, (i % 2) * 512:(i % 2) * 512 + 512],
                                        in0=stage3[:], in1=pb[6][:], op=ALU.mult)
                        v.tensor_copy(out=scrb[:, 2048 * 2 + i * 512:2048 * 2 + i * 512 + 512],
                                      in_=intt[:, (i % 2) * 512:(i % 2) * 512 + 512])
                        vmk[i] = sV.n

            for dm in range(8):
                @block.sync
                def _(sy, dm=dm):
                    waits(sy, sT)
                    if dm == 0:
                        waits(sy, sV)
                    sdma(sy, stage[:].bitcast(BF16), d_wdT1[dm, :, 0:2048])
                    sdma(sy, stage2[:].bitcast(BF16), d_wdT1[dm, :, 2048:4096])

                @block.tensor
                def _(t, dm=dm):
                    waits(t, sD, sS, sV)
                    for ic in range(32):
                        lhs = stage[:].bitcast(BF16) if ic < 16 else stage2[:].bitcast(BF16)
                        icc = ic % 16
                        t.matmul(pb[dm % 2][:], lhs[:, icc * 128:(icc + 1) * 128],
                                 scrb[:, 2048 * 2 + ic * 512:2048 * 2 + ic * 512 + 512],
                                 start=(ic == 0), stop=(ic == 31))
                    sT.inc(t.nop())

                @block.vector
                def _(v, dm=dm):
                    waits(v, sT, sSt)
                    v.tensor_copy(out=sq_sb[:], in_=pb[7][:])
                    sV.inc(v.tensor_mul(stage3[:], pb[dm % 2][:], sq_sb[:]))

                @block.scalar
                def _(s):
                    waits(s, sV)
                    sS.inc(s.activation(out=sq_sb[:], in_=stage3[:], func=AF.Sigmoid))

                @block.vector
                def _(v, dm=dm):
                    waits(v, sS)
                    sV.inc(v.tensor_mul(stage3[:], stage3[:], sq_sb[:]))

                @block.sync
                def _(sy, tb=tb, dm=dm):
                    waits(sy, sV)
                    stdma(sy, d_g1[dm, :, tb * 512:(tb + 1) * 512], stage3[:])

        QOF, KOF, VOF, VTA, WQKV, SCA, ZOF, EOF = 0, 1024, 2048, 3072, 4112, 5136, 7696, 9232
        for hg in range(8):
            @block.gpsimd
            def _(g, hg=hg):
                waits(g, sT, sS, sV)
                stgb = stage[:].bitcast(BF16)
                for wi, wn in enumerate(["L1_Wq", "L1_Wk", "L1_Wv"]):
                    dst = (stgb[:, 0:1024] if wi == 2 else
                           scrb[:, (WQKV + wi * 512) * 2:(WQKV + wi * 512) * 2 + 1024])
                    gdma(g, dst.rearrange("p (c f) -> p c f", c=8),
                         AP(w[wn].tensor, hg * 128, [[D, 128], [128 * D, 8], [1, 128]]))
                g.memset(scrb[:, VTA * 2:VTA * 2 + 2080], 1.0)
                sGc.inc(g.nop())

            for wi, OF in enumerate([QOF, KOF, VOF]):
                for tb in range(4):
                    @block.tensor
                    def _(t, wi=wi, tb=tb):
                        waits(t, sGc, sG, sS, sV)
                        stgb = stage[:].bitcast(BF16)
                        for ic in range(8):
                            lhs = (stgb[:, ic * 128:(ic + 1) * 128] if wi == 2 else
                                   scrb[:, (WQKV + wi * 512) * 2 + ic * 128:(WQKV + wi * 512) * 2 + (ic + 1) * 128])
                            t.matmul(pb[tb % 2][:], lhs,
                                     xnbf[:, ic * T + tb * 512: ic * T + tb * 512 + 512],
                                     start=(ic == 0), stop=(ic == 7))
                        sT.inc(t.nop())

                    @block.scalar
                    def _(s, OF=OF, tb=tb):
                        waits(s, sT)
                        sS.inc(s.copy(out=scrb[:, OF * 2 + tb * 512:OF * 2 + tb * 512 + 512],
                                      in_=pb[tb % 2][:]))

            for h in range(2):
                for g0 in range(0, 16, 8):
                    @block.tensor
                    def _(t, h=h, g0=g0):
                        waits(t, sS, sGc)
                        for j in range(8):
                            kc = g0 + j
                            t.transpose(pb[2 + j // 4][:, (j % 4) * 32:(j % 4) * 32 + 32].bitcast(BF16),
                                        scrb[:, VOF * 2 + kc * 128:VOF * 2 + (kc + 1) * 128][h * 64:(h + 1) * 64, :],
                                        identb[h * 64:(h + 1) * 64, h * 64:h * 64 + 64])
                        sT.inc(t.nop())

                    @block.scalar
                    def _(s, h=h, g0=g0):
                        waits(s, sT)
                        base = (VTA + h * 520) * 2
                        for j in range(8):
                            kc = g0 + j
                            s.copy(out=scrb[:, base + kc * 65:base + kc * 65 + 64],
                                   in_=pb[2 + j // 4][:, (j % 4) * 32:(j % 4) * 32 + 32].bitcast(BF16))
                        sS.inc(s.nop())

            for h in range(2):
                @block.sync
                def _(sy, hg=hg, h=h):
                    waits(sy, sV, sS)
                    sdma(sy, scr[:, SCA:SCA + SCW], w["L1_sc"][2 * hg + h])

                @block.vector
                def _(v):
                    waits(v, sD)
                    sV.inc(v.tensor_scalar(out=scr[:, SCA:SCA + SCW], in0=scr[:, SCA:SCA + SCW],
                                           scalar1=1.0 / 64.0, scalar2=-EXPC, op0=ALU.mult, op1=ALU.add))

                for qb in range(4):
                    kcmax = min(15, (qb * 512 + 511) // 128)
                    mark_sc, mark_z, mark_e = {}, {}, {}
                    CH = 4
                    st2b = stage2[:].bitcast(BF16)

                    def zsl(kc):
                        return scr[:, ZOF + (kc % 4) * 512:ZOF + (kc % 4) * 512 + 512]

                    def esl(kc):
                        return st2b[:, (kc % 4) * 512:(kc % 4) * 512 + 512]

                    def emit_avs(t, js, h, kcmax):
                        base = (VTA + h * 520) * 2
                        for j in js:
                            t.wait_ge(sS.h, mark_e[j])
                            t.matmul(pb[4][0:65, :], scrb[:, base + j * 65:base + (j + 1) * 65],
                                     esl(j), start=(j == 0), stop=(j == kcmax),
                                     skip_group_check=True)

                    for c0 in range(0, kcmax + 1, CH):
                        cN = min(c0 + CH, kcmax + 1)

                        @block.tensor
                        def _(t, c0=c0, cN=cN, h=h, qb=qb, kcmax=kcmax):
                            if c0 == 0:
                                waits(t, sS, sV)
                            emit_avs(t, range(max(0, c0 - CH), c0), h, kcmax)
                            for kc in range(c0, cN):
                                if kc - 4 in mark_z:
                                    t.wait_ge(sV.h, mark_z[kc - 4])
                                if kc - 4 in mark_e:
                                    t.wait_ge(sS.h, mark_e[kc - 4])
                                t.matmul(pb[kc % 4][:],
                                         scrb[:, KOF * 2 + kc * 128:KOF * 2 + (kc + 1) * 128][h * 64:(h + 1) * 64, :],
                                         scrb[:, QOF * 2 + qb * 512:QOF * 2 + (qb + 1) * 512][h * 64:(h + 1) * 64, :],
                                         start=True, stop=True)
                                mark_sc[kc] = sT.n

                        @block.vector
                        def _(v, c0=c0, cN=cN, h=h, qb=qb):
                            for kc in range(c0, cN):
                                v.wait_ge(sT.h, mark_sc[kc])
                                off = SCA + (qb * 512 - kc * 128) + 511
                                v.scalar_tensor_tensor(
                                    out=zsl(kc), in0=pb[kc % 4][:], scalar=1.0 / 64.0,
                                    in1=scr[:, off:off + 512], op0=ALU.mult, op1=ALU.add)
                                mark_z[kc] = sV.n

                        @block.scalar
                        def _(s, c0=c0, cN=cN):
                            for kc in range(c0, cN):
                                s.wait_ge(sV.h, mark_z[kc])
                                s.activation(out=esl(kc), in_=zsl(kc), func=AF.Exp)
                                mark_e[kc] = sS.n

                    @block.tensor
                    def _(t, h=h, kcmax=kcmax):
                        last0 = (kcmax // CH) * CH
                        emit_avs(t, range(last0, kcmax + 1), h, kcmax)
                        sT.inc(t.nop())

                    @block.vector
                    def _(v):
                        waits(v, sT)
                        recip_newton(v, pb[4][64:65, 0:512], rowa[0:1, 0:512], rowb[0:1, 0:512])
                        sV.inc(v.nop())

                    @block.tensor
                    def _(t):
                        waits(t, sV)
                        t.matmul(pb[5][0:64, :], ones_r[0:1, 0:64], rowa[0:1, 0:512], start=True, stop=True)
                        sT.inc(t.nop())

                    @block.vector
                    def _(v, hg=hg, h=h, qb=qb):
                        waits(v, sT)
                        v.tensor_copy(out=stage3[0:64, :], in_=pb[5][0:64, :])
                        sV.inc(v.tensor_tensor(
                            out=xqbf[h * 64:(h + 1) * 64, hg * T + qb * 512:hg * T + qb * 512 + 512],
                            in0=pb[4][0:64, :], in1=stage3[0:64, :], op=ALU.mult))

        for tb in range(4):
            for dm in range(8):
                @block.gpsimd
                def _(g, dm=dm):
                    waits(g, sT)
                    gdma(g, scrb[:, 8192 + (dm % 2) * 1024:8192 + (dm % 2 + 1) * 1024]
                         .rearrange("p (c f) -> p c f", c=8),
                         AP(w["L1_Wo"].tensor, dm * 128, [[D, 128], [128 * D, 8], [1, 128]]))

                @block.sync
                def _(sy, tb=tb, dm=dm):
                    waits(sy, sV)
                    sdma(sy, scr[:, 5376 + (dm % 2) * 512:5376 + (dm % 2) * 512 + 512],
                         d_g1[dm, :, tb * 512:(tb + 1) * 512])

                @block.tensor
                def _(t, tb=tb, dm=dm):
                    waits(t, sG, sV)
                    for ic in range(8):
                        t.matmul(pb[dm % 2][:],
                                 scrb[:, 8192 + (dm % 2) * 1024 + ic * 128:8192 + (dm % 2) * 1024 + (ic + 1) * 128],
                                 xqbf[:, ic * T + tb * 512:ic * T + tb * 512 + 512],
                                 start=(ic == 0), stop=(ic == 7))
                    sT.inc(t.nop())

                @block.vector
                def _(v, tb=tb, dm=dm):
                    waits(v, sT, sD, sS)
                    d0 = dm * T + tb * 512
                    v.tensor_scalar(out=stage3[:], in0=pb[dm % 2][:], scalar1=bo_sb[:, dm:dm + 1],
                                    scalar2=None, op0=ALU.add)
                    v.tensor_mul(stage3[:], stage3[:], scr[:, 5376 + (dm % 2) * 512:5376 + (dm % 2) * 512 + 512])
                    sV.inc(v.tensor_add(x1[:, d0:d0 + 512], x1[:, d0:d0 + 512], stage3[:]))

        if DEBUG_DUMP is not None:
            if DEBUG_DUMP in ("xnbf", "xqbf"):
                sb_t = {"xnbf": xnbf, "xqbf": xqbf}[DEBUG_DUMP]

                @block.gpsimd
                def _(g):
                    waits(g, sSt, sV, sS, sT, sD)
                    gdma(g, AP(dout.tensor, 0, [[16384, 128], [1, 16384]]), sb_t[:])
                    waits(g, sG)
                return nc
            dt_map = {"d_g1": d_g1, "d_gate1": d_gate1, "d_up1": d_up1, "d_g": d_g,
                      "d_ssm": d_ssm, "d_xn": d_xn, "d_wdT1": d_wdT1}
            src_t = dt_map[DEBUG_DUMP]

            @block.sync
            def _(sy):
                waits(sy, sSt, sV, sS, sT)
                sdma(sy, AP(dout.tensor, 0, [[1, 2048 * 1024]]),
                     AP(src_t.tensor, 0, [[1, 2048 * 1024]]))
                waits(sy, sD, sSt)
            return nc

        emit_output(x1)

        @block.sync
        def _(sy):
            waits(sy, sSt)

    return nc


_NC_CACHE = {}


def _get_nc():
    if "nc" not in _NC_CACHE:
        nc = bass.Bass("TRN2", target_bir_lowering=False)
        _NC_CACHE["nc"] = build(nc)
    return _NC_CACHE["nc"]


def kernel(L0_attn_x, L0_attn_ls_scale, L0_attn_rs_scale, L0_attn_split_scale,
           L0_attn_Wk, L0_attn_Wugg, L0_attn_Wout,
           L0_ffn_w_gate, L0_ffn_w_up, L0_ffn_w_down,
           L1_attn_Wq, L1_attn_Wk, L1_attn_Wv, L1_attn_Wo, L1_attn_bo,
           L1_attn_rel_pos_bias,
           L1_ffn_w_gate, L1_ffn_w_up, L1_ffn_w_down):
    x = np.ascontiguousarray(np.asarray(L0_attn_x, dtype=np.float32))
    B = x.shape[0]

    rpb = np.ascontiguousarray(np.asarray(L1_attn_rel_pos_bias, dtype=np.float32))
    p = np.arange(128)[:, None]
    m = np.arange(SCW)[None, :]
    dgrid = m - p - 511
    idx = np.clip(1023 - dgrid, 0, 1023)
    sc = rpb[:, idx]
    sc[:, dgrid < 0] = -1e9
    sc = np.ascontiguousarray(sc, dtype=np.float32)

    common = {
        "L0_ls": np.ascontiguousarray(L0_attn_ls_scale, dtype=np.float32),
        "L0_rs": np.ascontiguousarray(L0_attn_rs_scale, dtype=np.float32),
        "L0_ss": np.ascontiguousarray(L0_attn_split_scale, dtype=np.float32),
        "L0_Wk": np.ascontiguousarray(L0_attn_Wk, dtype=np.float32),
        "L0_Wugg": np.ascontiguousarray(L0_attn_Wugg, dtype=np.float32),
        "L0_Wout": np.ascontiguousarray(L0_attn_Wout, dtype=np.float32),
        "L0_wg": np.ascontiguousarray(L0_ffn_w_gate, dtype=np.float32),
        "L0_wu": np.ascontiguousarray(L0_ffn_w_up, dtype=np.float32),
        "L0_wd": np.ascontiguousarray(L0_ffn_w_down, dtype=np.float32),
        "L1_Wq": np.ascontiguousarray(L1_attn_Wq, dtype=np.float32),
        "L1_Wk": np.ascontiguousarray(L1_attn_Wk, dtype=np.float32),
        "L1_Wv": np.ascontiguousarray(L1_attn_Wv, dtype=np.float32),
        "L1_Wo": np.ascontiguousarray(L1_attn_Wo, dtype=np.float32),
        "L1_bo": np.ascontiguousarray(L1_attn_bo, dtype=np.float32),
        "L1_sc": sc,
        "L1_wg": np.ascontiguousarray(L1_ffn_w_gate, dtype=np.float32),
        "L1_wu": np.ascontiguousarray(L1_ffn_w_up, dtype=np.float32),
        "L1_wd": np.ascontiguousarray(L1_ffn_w_down, dtype=np.float32),
    }
    in_maps = [dict(common, x=np.ascontiguousarray(x[i])) for i in range(B)]
    nc = _get_nc()
    res = run_bass_kernel_spmd(nc, in_maps, core_ids=list(range(B)))
    return np.stack([res.results[i]["out"] for i in range(B)], axis=0)


# version 18 (final)
# speedup vs baseline: 1.0165x; 1.0165x over previous
import numpy as np
from contextlib import ExitStack

import concourse.bass as bass
import concourse.mybir as mybir
from concourse.ap import AP
from concourse.bass_utils import run_bass_kernel_spmd

F32 = mybir.dt.float32
BF16 = mybir.dt.bfloat16
I32 = mybir.dt.int32
AF = mybir.ActivationFunctionType
ALU = mybir.AluOpType
AX = mybir.AxisListType

T = 2048
D = 1024
STATE = 2048
FFN0 = 3072
FFN1 = 4096
NH = 16
EXPC = 20.0
SCW = 2560

DEBUG_STOP = None
DEBUG_N = 16
DEBUG_PART = 9
DEBUG_DUMP = None



_EMIT = {
    "tensor_copy", "tensor_tensor", "tensor_add", "tensor_sub", "tensor_mul",
    "tensor_max", "tensor_scalar", "tensor_single_scalar", "tensor_scalar_add",
    "tensor_scalar_sub", "tensor_scalar_mul", "tensor_scalar_max",
    "tensor_scalar_min", "tensor_relu", "tensor_reduce", "reduce_max",
    "reduce_sum", "scalar_tensor_tensor", "tensor_tensor_scan", "reciprocal",
    "memset", "memzero", "iota", "affine_select", "activation", "copy",
    "select", "transpose", "bn_stats", "bn_aggr", "engine_nop",
}

_OUT0 = {
    "tensor_add", "tensor_sub", "tensor_mul", "tensor_max", "tensor_scalar_add",
    "tensor_scalar_sub", "tensor_scalar_mul", "tensor_scalar_max",
    "tensor_scalar_min", "reciprocal", "memset", "memzero", "tensor_relu",
}


def _ap_name(x):
    try:
        return x.name
    except Exception:
        return None


class _W:

    def __init__(self, eng, sem):
        self._e = eng
        self._m = sem

    def __getattr__(self, name):
        fn = getattr(self._e, name)
        if name not in _EMIT:
            return fn

        def wrap(*a, **k):
            reads, writes = set(), set()
            for i, x in enumerate(a):
                n = _ap_name(x)
                if n is None:
                    continue
                (writes if (i == 0 and name in _OUT0) else reads).add(n)
            for kk, x in k.items():
                n = _ap_name(x)
                if n is None:
                    continue
                (writes if kk in ("out", "accum_out", "out_ap") else reads).add(n)
            pend = self._m.pend
            if (reads & pend) or (writes & pend) or (writes & self._m.rd):
                self._e.wait_ge(self._m.h, self._m.n)
                pend.clear()
                self._m.rd.clear()
            ins = fn(*a, **k)
            ins.then_inc(self._m.h, 1)
            self._m.n += 1
            self._m.last = ins
            pend.update(writes)
            self._m.rd.update(reads)
            return ins
        return wrap

class _WT:

    def __init__(self, eng, sem):
        self._e = eng
        self._m = sem

    def __getattr__(self, name):
        fn = getattr(self._e, name)
        if name not in ("matmul", "transpose"):
            return fn

        def wrap(*a, **k):
            ins = fn(*a, **k)
            ins.then_inc(self._m.h, 1)
            self._m.n += 1
            self._m.last = ins
            return ins
        return wrap


class _WBlock:
    def __init__(self, b, sems):
        self._b = b
        self._sems = sems

    def vector(self, f):
        return self._b.vector(lambda e, __f=f: __f(_W(e, self._sems["v"])))

    def scalar(self, f):
        return self._b.scalar(lambda e, __f=f: __f(_W(e, self._sems["s"])))

    def gpsimd(self, f):
        return self._b.gpsimd(lambda e, __f=f: __f(_W(e, self._sems["g"])))

    def tensor(self, f):
        return self._b.tensor(lambda e, __f=f: __f(_WT(e, self._sems["t"])))

    def sync(self, f):
        return self._b.sync(f)


class Sem:
    def __init__(self, h):
        self.h = h
        self.n = 0
        self.last = None
        self.pend = set()
        self.rd = set()

    def inc(self, instr, k=1):
        if instr is self.last:
            return
        instr.then_inc(self.h, k)
        self.n += k


def build(nc: bass.Bass):
    dx = nc.dram_tensor("x", [T, D], F32, kind="ExternalInput").ap()
    w = {}
    for name, shape in [
        ("L0_ls", [512]), ("L0_rs", [512]), ("L0_ss", [D]),
        ("L0_Wk", [D, STATE]), ("L0_Wugg", [D, 3 * STATE]), ("L0_Wout", [STATE, D]),
        ("L0_wg", [FFN0, D]), ("L0_wu", [FFN0, D]), ("L0_wd", [D, FFN0]),
        ("L1_Wq", [D, D]), ("L1_Wk", [D, D]), ("L1_Wv", [D, D]), ("L1_Wo", [D, D]),
        ("L1_bo", [D]), ("L1_sc", [NH, 128, SCW]),
        ("L1_wg", [FFN1, D]), ("L1_wu", [FFN1, D]), ("L1_wd", [D, FFN1]),
    ]:
        w[name] = nc.dram_tensor(name, shape, F32, kind="ExternalInput").ap()
    dout = nc.dram_tensor("out", [T, D], F32, kind="ExternalOutput").ap()

    d_xn = nc.dram_tensor("d_xn", [16, 8, 128, 128], F32).ap()
    d_ssm = nc.dram_tensor("d_ssm", [16, 128, T], BF16).ap()
    d_up = nc.dram_tensor("d_up", [24, 128, T], F32).ap()
    d_gate = nc.dram_tensor("d_gate", [24, 128, T], F32).ap()
    d_g2 = nc.dram_tensor("d_g2", [24, 128, T], F32).ap()
    d_o = nc.dram_tensor("d_o", [24, 128, T], F32).ap()
    d_g = nc.dram_tensor("d_g", [8, 128, T], F32).ap()
    d_gate1 = nc.dram_tensor("d_gate1", [32, 128, T], F32).ap()
    d_up1 = nc.dram_tensor("d_up1", [32, 128, T], F32).ap()
    d_g1 = nc.dram_tensor("d_g1", [8, 128, T], F32).ap()
    d_wdT1 = nc.dram_tensor("d_wdT1", [8, 128, FFN1], BF16).ap()
    cc_in = nc.dram_tensor("cc_in", [1, 8], F32)
    cc_out = nc.dram_tensor("cc_out", [1, 8], F32)

    st = ExitStack()
    sbt = lambda name, shape, dt=F32: st.enter_context(nc.sbuf_tensor(name, shape, dt))

    with st:
        xnbf = sbt("xnbf", [128, 8 * T], BF16)
        xqbf = sbt("xqbf", [128, 8 * T], BF16)
        x1 = sbt("x1", [128, 8 * T])
        SCR = 10240
        scr = sbt("scr", [128, SCR])
        scrb = scr[:].bitcast(BF16)
        ident = sbt("ident", [128, 128])
        identb = sbt("identb", [128, 128], BF16)
        ones_r = sbt("ones_r", [1, 128])
        ones_c = sbt("ones_c", [128, 1])
        fs0 = sbt("fs0", [128, 8])
        bo_sb = sbt("bo_sb", [128, 8])
        stats = sbt("stats", [128, 64])
        blks = sbt("blks", [128, 288])
        rowa = sbt("rowa", [1, 1024])
        rowb = sbt("rowb", [1, 1024])
        rowq = sbt("rowq", [1, 2048])
        rowc = sbt("rowc", [1, 64])
        cc_sb = sbt("cc_sb", [1, 8])
        stage = sbt("stage", [128, 1024])
        stage2 = sbt("stage2", [128, 1024])
        stage3 = sbt("stage3", [128, 512])
        stage4 = stage3
        xnt = sbt("xnt", [128, 1024])
        sq_sb = sbt("sq_sb", [128, 512])
        intt = sbt("intt", [128, 1024], I32)
        pb = [st.enter_context(nc.psum_tensor(f"pb{i}", [128, 512], F32)) for i in range(8)]

        sD = Sem(st.enter_context(nc.semaphore("sD")))
        sG = Sem(st.enter_context(nc.semaphore("sG")))
        sGc = Sem(st.enter_context(nc.semaphore("sGc")))
        sT = Sem(st.enter_context(nc.semaphore("sT")))
        sV = Sem(st.enter_context(nc.semaphore("sV")))
        sS = Sem(st.enter_context(nc.semaphore("sS")))
        sSt = Sem(st.enter_context(nc.semaphore("sSt")))
        sCC = Sem(st.enter_context(nc.semaphore("sCC")))
        sDp = [Sem(st.enter_context(nc.semaphore(f"sDp{i}"))) for i in range(2)]
        sGp = [Sem(st.enter_context(nc.semaphore(f"sGp{i}"))) for i in range(2)]
        sSp = [Sem(st.enter_context(nc.semaphore(f"sSp{i}"))) for i in range(2)]

        def pdma(sy, par, out, in_):
            sy.dma_start(out=out, in_=in_).then_inc(sDp[par].h, 16)
            sDp[par].n += 16
            return sDp[par].n

        def pgdma(g, par, out, in_):
            g.dma_start(out=out, in_=in_).then_inc(sGp[par].h, 16)
            sGp[par].n += 16
            return sGp[par].n

        def pstdma(sy, par, out, in_):
            sy.dma_start(out=out, in_=in_).then_inc(sSp[par].h, 16)
            sSp[par].n += 16
            return sSp[par].n

        def drain_parity(eng):
            for s in (sDp[0], sDp[1], sGp[0], sGp[1], sSp[0], sSp[1]):
                if s.n > 0:
                    eng.wait_ge(s.h, s.n)

        block = _WBlock(st.enter_context(nc.Block()), {"v": sV, "s": sS, "g": sGc, "t": sT})

        ALL = (sD, sG, sGc, sT, sV, sS, sSt)

        def waits(eng, *sems):
            for s in sems:
                if s.n > 0:
                    eng.wait_ge(s.h, s.n)

        def recip_newton(v, src, dst, tmp):
            v.reciprocal(dst, src)
            v.tensor_mul(tmp, src, dst)
            v.tensor_scalar(out=tmp, in0=tmp, scalar1=-1.0, scalar2=2.0, op0=ALU.mult, op1=ALU.add)
            v.tensor_mul(dst, dst, tmp)

        def scale_cols(v, src, scol, rcol, qmax):
            v.tensor_scalar(out=stats[:, scol:scol + 1], in0=src, scalar1=1.0 / qmax,
                            scalar2=1e-12, op0=ALU.mult, op1=ALU.max)
            recip_newton(v, stats[:, scol:scol + 1], stats[:, rcol:rcol + 1], stats[:, 19:20])

        def part_absmax(col, cccol):
            @block.tensor
            def _(t):
                waits(t, sV, sS)
                t.transpose(pb[0][0:1, 0:128], stats[:, col:col + 1], ident[:])
                sT.inc(t.nop())

            @block.vector
            def _(v):
                waits(v, sT)
                sV.inc(v.tensor_reduce(out=cc_sb[0:1, cccol:cccol + 1], in_=pb[0][0:1, 0:128],
                                       axis=AX.X, op=ALU.max, apply_absolute_value=True))

        def allreduce_max():
            @block.gpsimd
            def _(g):
                waits(g, sV)
                g.dma_start(out=cc_in.ap(), in_=cc_sb[:]).then_inc(sG.h, 16)
                sG.n += 16
                waits(g, sG)
                sCC.inc(g.collective_compute(
                    "AllReduce", ALU.max, replica_groups=[list(range(8))],
                    ins=[cc_in.ap().opt()], outs=[cc_out.ap().opt()]))
                waits(g, sCC)
                g.dma_start(out=cc_sb[:], in_=cc_out.ap()).then_inc(sG.h, 16)
                sG.n += 16

        def bcast_cc(col, bank):
            @block.tensor
            def _(t):
                waits(t, sG, sV, sS)
                t.matmul(pb[bank][:, 0:1], ones_r[:], cc_sb[0:1, col:col + 1], start=True, stop=True)
                sT.inc(t.nop())

        def gdma(g, out, in_):
            g.dma_start(out=out, in_=in_).then_inc(sG.h, 16)
            sG.n += 16

        def sdma(sy, out, in_, **kw):
            sy.dma_start(out=out, in_=in_, **kw).then_inc(sD.h, 16)
            sD.n += 16

        def stdma(sy, out, in_):
            sy.dma_start(out=out, in_=in_).then_inc(sSt.h, 16)
            sSt.n += 16

        def emit_dummy_output():
            @block.gpsimd
            def _(g):
                waits(g, sT, sV, sS, sD, sG, sSt)
                sGc.inc(g.memset(stage[:], 0.0))

            @block.sync
            def _(sy):
                waits(sy, sGc, sSt)
                for tt in range(16):
                    stdma(sy, dout[tt * 128:(tt + 1) * 128, :], stage[:])
                waits(sy, sSt)

        @block.gpsimd
        def _(g):
            sGc.inc(g.memset(ident[:], 0.0))
            sGc.inc(g.memset(identb[:], 0.0))
            g.wait_ge(sGc.h, sGc.n)
            g.affine_select(out=ident[:], in_=ident[:], compare_op=ALU.not_equal,
                            fill=1.0, base=0, pattern=[[-1, 128]], channel_multiplier=1)
            g.affine_select(out=identb[:], in_=identb[:], compare_op=ALU.not_equal,
                            fill=1.0, base=0, pattern=[[-1, 128]], channel_multiplier=1)
            g.memset(ones_r[:], 1.0)
            g.memset(ones_c[:], 1.0)
            g.memset(cc_sb[:], 0.0)
            sGc.inc(g.memset(stats[:], 0.0))

        @block.sync
        def _(sy):
            sdma(sy, stage[:, 0:4], AP(w["L0_ls"].tensor, 0, [[1, 128], [128, 4]]), allow_slow_non_contiguous=True)
            sdma(sy, stage[:, 4:8], AP(w["L0_rs"].tensor, 0, [[1, 128], [128, 4]]), allow_slow_non_contiguous=True)
            sdma(sy, stage[:, 8:16], AP(w["L0_ss"].tensor, 0, [[1, 128], [128, 8]]), allow_slow_non_contiguous=True)
            sdma(sy, bo_sb[:], AP(w["L1_bo"].tensor, 0, [[1, 128], [128, 8]]), allow_slow_non_contiguous=True)

        @block.vector
        def _(v):
            waits(v, sD, sGc)
            v.tensor_mul(fs0[:, 0:4], stage[:, 0:4], stage[:, 8:12])
            sV.inc(v.tensor_mul(fs0[:, 4:8], stage[:, 4:8], stage[:, 12:16]))

        ld_m = {}
        for tt in range(16):
            xt = [stage, stage2][tt % 2]

            @block.sync
            def _(sy, tt=tt, xt=xt):
                waits(sy, sV, sS, sSt)
                ld_m[tt] = pdma(sy, tt % 2, xt[:], dx[tt * 128:(tt + 1) * 128, :])

            @block.vector
            def _(v, xt=xt, tt=tt):
                v.wait_ge(sDp[tt % 2].h, ld_m[tt])
                waits(v, sS)
                v.reduce_sum(stats[:, 8:9], xt[:], axis=AX.X)
                v.tensor_scalar_mul(stats[:, 9:10], stats[:, 8:9], 1.0 / D)
                v.tensor_scalar_mul(stats[:, 10:11], stats[:, 9:10], -1.0)
                v.tensor_mul(stats[:, 11:12], stats[:, 9:10], stats[:, 9:10])
                sV.inc(v.tensor_scalar(out=stats[:, 12:13], in0=stats[:, 11:12],
                                       scalar1=-1.0, scalar2=1e-5, op0=ALU.mult, op1=ALU.add))

            @block.scalar
            def _(s, xt=xt, tt=tt):
                s.wait_ge(sDp[tt % 2].h, ld_m[tt])
                waits(s, sV)
                s.activation(out=sq_sb[:], in_=xt[:, 0:512], func=AF.Square, accum_out=stats[:, 13:14])
                s.activation(out=sq_sb[:], in_=xt[:, 512:1024], func=AF.Square, accum_out=stats[:, 14:15])
                sS.inc(s.nop())

            @block.vector
            def _(v):
                waits(v, sS)
                sV.inc(v.tensor_add(stats[:, 15:16], stats[:, 13:14], stats[:, 14:15]))

            @block.scalar
            def _(s):
                waits(s, sV)
                sS.inc(s.activation(out=stats[:, 24:25], in_=stats[:, 15:16], func=AF.Sqrt,
                                    scale=1.0 / D, bias=stats[:, 12:13]))

            @block.vector
            def _(v, xt=xt):
                waits(v, sS, sT)
                recip_newton(v, stats[:, 24:25], stats[:, 16:17], stats[:, 19:20])
                v.tensor_scalar(out=xnt[:], in0=xt[:], scalar1=stats[:, 10:11],
                                scalar2=stats[:, 16:17], op0=ALU.add, op1=ALU.mult)
                v.tensor_reduce(out=stats[:, 17:18], in_=xnt[:], axis=AX.X,
                                op=ALU.max, apply_absolute_value=True)
                sV.inc(v.tensor_max(stats[:, 0:1], stats[:, 0:1], stats[:, 17:18]))

            @block.scalar
            def _(s):
                waits(s, sV)
                s.activation(out=sq_sb[:], in_=xnt[:, 0:512], func=AF.Square, accum_out=stats[:, 13:14])
                s.activation(out=sq_sb[:], in_=xnt[:, 512:1024], func=AF.Square, accum_out=stats[:, 14:15])
                s.activation(out=stats[:, 20:21], in_=stats[:, 13:14], func=AF.Sqrt, scale=1.0 / 512.0)
                sS.inc(s.activation(out=stats[:, 21:22], in_=stats[:, 14:15], func=AF.Sqrt, scale=1.0 / 512.0))

            @block.vector
            def _(v, xt=xt):
                waits(v, sS)
                v.tensor_scalar_add(stats[:, 20:21], stats[:, 20:21], 1e-8)
                v.tensor_scalar_add(stats[:, 21:22], stats[:, 21:22], 1e-8)
                recip_newton(v, stats[:, 20:21], stats[:, 22:23], stats[:, 19:20])
                recip_newton(v, stats[:, 21:22], stats[:, 23:24], stats[:, 19:20])
                v.tensor_scalar(out=xt[:, 0:512], in0=xnt[:, 0:512],
                                scalar1=stats[:, 22:23], scalar2=None, op0=ALU.mult)
                sV.inc(v.tensor_scalar(out=xt[:, 512:1024], in0=xnt[:, 512:1024],
                                       scalar1=stats[:, 23:24], scalar2=None, op0=ALU.mult))

            @block.tensor
            def _(t, xt=xt):
                waits(t, sV, sS)
                for c in range(8):
                    t.transpose(pb[4 + c // 4][:, (c % 4) * 128:(c % 4) * 128 + 128],
                                xnt[:, c * 128:(c + 1) * 128], ident[:])
                for c in range(8):
                    t.transpose(pb[6 + c // 4][:, (c % 4) * 128:(c % 4) * 128 + 128],
                                xt[:, c * 128:(c + 1) * 128], ident[:])
                sT.inc(t.nop())

            @block.scalar
            def _(s, xt=xt):
                waits(s, sT, sSt)
                for c in range(8):
                    s.copy(out=xt[:, c * 128:(c + 1) * 128],
                           in_=pb[4 + c // 4][:, (c % 4) * 128:(c % 4) * 128 + 128])
                sS.inc(s.nop())

            @block.vector
            def _(v, tt=tt):
                waits(v, sT)
                for c in range(8):
                    v.tensor_scalar(out=xnbf[:, c * T + tt * 128: c * T + (tt + 1) * 128],
                                    in0=pb[6 + c // 4][:, (c % 4) * 128:(c % 4) * 128 + 128],
                                    scalar1=fs0[:, c:c + 1], scalar2=None, op0=ALU.mult)
                sV.inc(v.nop())

            @block.sync
            def _(sy, tt=tt, xt=xt):
                waits(sy, sS)
                stdma(sy, AP(d_xn.tensor, tt * 8 * 128 * 128, [[128, 128], [128 * 128, 8], [1, 128]]),
                      xt[:].rearrange("p (c f) -> p c f", c=8))

        if DEBUG_STOP == "p1":
            emit_dummy_output()
            return nc

        part_absmax(0, 0)
        allreduce_max()

        if DEBUG_STOP == "mx":
            emit_dummy_output()
            return nc

        KS, U2, GO, HH = 0, 2048, 4096, 6144

        def wt0(par, q):
            base = 8192 * 2 + q * 1024
            return scrb[:, base:base + 1024]

        wugg_cols = [None, 0, STATE, 2 * STATE]
        stgb0 = stage[:].bitcast(BF16)
        stgb1 = stage2[:].bitcast(BF16)

        def wt0(par, q):
            if par == 0:
                return scrb[:, 8192 * 2 + q * 1024:8192 * 2 + (q + 1) * 1024]
            if q < 2:
                return stgb0[:, q * 1024:(q + 1) * 1024]
            return stgb1[:, (q - 2) * 1024:(q - 1) * 1024]

        p2_ld, p2_mm, p2_ev, p2_vc, p2_scan, p2_st = {}, {}, {}, {}, {}, {}
        for sc in range(DEBUG_N):
            par = sc % 2

            @block.gpsimd
            def _(g, sc=sc, par=par):
                if sc == 0:
                    waits(g, sT, sS, sV, sSt)
                    drain_parity(g)
                if (sc - 2, 3) in p2_mm:
                    g.wait_ge(sT.h, p2_mm[(sc - 2, 3)])
                pgdma(g, par, wt0(par, 0).rearrange("p (c f) -> p c f", c=8),
                      AP(w["L0_Wk"].tensor, sc * 128, [[STATE, 128], [128 * STATE, 8], [1, 128]]))
                for q in (1, 2, 3):
                    pgdma(g, par, wt0(par, q).rearrange("p (c f) -> p c f", c=8),
                          AP(w["L0_Wugg"].tensor, wugg_cols[q] + sc * 128,
                             [[3 * STATE, 128], [128 * 3 * STATE, 8], [1, 128]]))
                p2_ld[sc] = sGp[par].n

            for tb in range(4):
                @block.tensor
                def _(t, sc=sc, tb=tb, par=par):
                    t.wait_ge(sGp[par].h, p2_ld[sc])
                    if sc == 0 and tb == 0:
                        waits(t, sV, sS)
                    key = sc * 4 + tb - 2
                    if key >= 0:
                        t.wait_ge(sS.h, p2_ev[key][0])
                        t.wait_ge(sV.h, p2_ev[key][1])
                    bs = 4 * (tb % 2)
                    for q in range(4):
                        for c in range(8):
                            t.matmul(pb[bs + q][:],
                                     wt0(par, q)[:, c * 128:(c + 1) * 128],
                                     xnbf[:, c * T + tb * 512: c * T + tb * 512 + 512],
                                     start=(c == 0), stop=(c == 7))
                    p2_mm[(sc, tb)] = sT.n

                @block.scalar
                def _(s, sc=sc, tb=tb):
                    s.wait_ge(sT.h, p2_mm[(sc, tb)])
                    if sc >= 1:
                        s.wait_ge(sV.h, p2_scan[sc - 1])
                    if tb <= 1 and sc >= 2:
                        s.wait_ge(sSt.h, p2_st[sc - 1])
                    bs = 4 * (tb % 2)
                    s.activation(out=scr[:, KS + tb * 512: KS + tb * 512 + 512], in_=pb[bs][:], func=AF.Sigmoid)
                    s.activation(out=scr[:, HH + tb * 512: HH + tb * 512 + 512], in_=pb[bs + 2][:], func=AF.Sigmoid)
                    s.activation(out=scr[:, GO + tb * 512: GO + tb * 512 + 512], in_=pb[bs + 3][:], func=AF.Sigmoid)
                    p2_ev[sc * 4 + tb] = [sS.n, None]

                @block.vector
                def _(v, sc=sc, tb=tb):
                    v.wait_ge(sS.h, p2_ev[sc * 4 + tb][0])
                    if tb <= 1 and sc >= 1:
                        v.wait_ge(sSt.h, p2_st.get(sc - 1, 0) or 0) if False else None
                    if tb <= 1 and sc >= 1 and (sc - 1) in p2_st:
                        v.wait_ge(sSt.h, p2_st[sc - 1])
                    bs = 4 * (tb % 2)
                    sl = slice(U2 + tb * 512, U2 + tb * 512 + 512)
                    hs = slice(HH + tb * 512, HH + tb * 512 + 512)
                    ks = slice(KS + tb * 512, KS + tb * 512 + 512)
                    v.tensor_mul(scr[:, sl], pb[bs + 1][:], scr[:, hs])
                    v.tensor_mul(scr[:, hs], scr[:, sl], scr[:, ks])
                    v.tensor_sub(scr[:, sl], scr[:, sl], scr[:, hs])
                    p2_ev[sc * 4 + tb][1] = sV.n

            @block.vector
            def _(v, sc=sc):
                v.tensor_copy(out=scr[:, HH:HH + 1], in_=scr[:, U2:U2 + 1])
                v.tensor_tensor_scan(out=scr[:, HH + 1:HH + T], data0=scr[:, KS:KS + T - 1],
                                     data1=scr[:, U2 + 1:U2 + T], initial=scr[:, HH:HH + 1],
                                     op0=ALU.mult, op1=ALU.add)
                v.tensor_mul(scrb[:, U2 * 2:U2 * 2 + T], scr[:, HH:HH + T], scr[:, GO:GO + T])
                p2_scan[sc] = sV.n

            @block.sync
            def _(sy, sc=sc):
                sy.wait_ge(sV.h, p2_scan[sc])
                stdma(sy, d_ssm[sc], scrb[:, U2 * 2:U2 * 2 + T])
                p2_st[sc] = sSt.n

        if DEBUG_STOP == "p2":
            emit_dummy_output()
            return nc

        bcast_cc(0, 0)

        @block.vector
        def _(v):
            waits(v, sT)
            scale_cols(v, pb[0][:, 0:1], 2, 3, 32767.0)
            sV.inc(v.nop())

        xq_ld = {}
        for c in range(8):
            @block.sync
            def _(sy, c=c):
                waits(sy, sV, sS, sSt)
                if c == 0:
                    drain_parity(sy)
                xq_ld[c] = pdma(sy, c % 2, scr[:, (c % 2) * 2048:(c % 2) * 2048 + 2048].rearrange("p (a f) -> p a f", a=16),
                                AP(d_xn.tensor, c * 128 * 128, [[128, 128], [8 * 128 * 128, 16], [1, 128]]))

            @block.scalar
            def _(s, c=c):
                s.wait_ge(sDp[c % 2].h, xq_ld[c])
                waits(s, sV)
                for hh in range(2):
                    s.activation(out=intt[:], in_=scr[:, (c % 2) * 2048 + hh * 1024:(c % 2) * 2048 + (hh + 1) * 1024],
                                 func=AF.Copy, scale=stats[:, 3:4])
                    s.activation(out=xqbf[:, c * T + hh * 1024:c * T + (hh + 1) * 1024], in_=intt[:],
                                 func=AF.Copy, scale=stats[:, 2:3])
                sS.inc(s.nop())

        if DEBUG_STOP == "p3":
            emit_dummy_output()
            return nc


        def transpose_evac(nblocks, src_bf16_off, dst_slice_fn):
            for g0 in range(0, nblocks, 8):
                gn = min(8, nblocks - g0)

                @block.tensor
                def _(t, g0=g0, gn=gn):
                    waits(t, sV, sS)
                    for j in range(gn):
                        ic = g0 + j
                        t.transpose(pb[6 + j // 4][:, (j % 4) * 64:(j % 4) * 64 + 64].bitcast(BF16),
                                    scrb[:, src_bf16_off + ic * 128:src_bf16_off + (ic + 1) * 128],
                                    identb[:])
                    sT.inc(t.nop())

                @block.scalar
                def _(s, g0=g0, gn=gn):
                    waits(s, sT)
                    for j in range(gn):
                        s.copy(out=dst_slice_fn(g0 + j),
                               in_=pb[6 + j // 4][:, (j % 4) * 64:(j % 4) * 64 + 64].bitcast(BF16))
                    sS.inc(s.nop())

        p4q, p4tr, p4ev, p4ld = {}, {}, {}, {}
        p4mm, p4mmev, p4st, p4mm4 = {}, {}, {}, {}
        for idx in range(48):
            wname, ddst, mcol = [("L0_wu", d_up, 5), ("L0_wg", d_gate, 6)][idx // 24]
            oc = idx % 24
            WR = (idx % 2) * 1024
            WQT = 3584 + (idx % 2) * 512

            @block.sync
            def _(sy, wname=wname, oc=oc, WR=WR, idx=idx):
                if idx == 0:
                    waits(sy, sV, sS, sT)
                    drain_parity(sy)
                if idx - 2 in p4q:
                    sy.wait_ge(sV.h, p4q[idx - 2])
                p4ld[idx] = pdma(sy, idx % 2, scr[:, WR:WR + 1024], w[wname][oc * 128:(oc + 1) * 128, :])

            @block.scalar
            def _(s, WR=WR, idx=idx):
                s.wait_ge(sDp[idx % 2].h, p4ld[idx])
                if idx - 1 in p4q:
                    s.wait_ge(sV.h, p4q[idx - 1])
                s.activation(out=scr[:, 2048:3072], in_=scr[:, WR:WR + 1024], func=AF.Abs)
                sabs = sS.n

            sabs = sS.n

            @block.vector
            def _(v, WR=WR, idx=idx):
                v.wait_ge(sS.h, sabs)
                if idx - 1 in p4tr:
                    v.wait_ge(sT.h, p4tr[idx - 1])
                nblk = 32
                v.tensor_reduce(out=blks[:, 0:nblk].rearrange("p (b o) -> p b o", o=1),
                                in_=scr[:, 2048:3072].rearrange("p (b k) -> p b k", k=32),
                                axis=AX.X, op=ALU.max)
                v.tensor_scalar(out=blks[:, 96:96 + nblk], in0=blks[:, 0:nblk],
                                scalar1=1.0 / 7.0, scalar2=1e-12, op0=ALU.mult, op1=ALU.max)
                recip_newton(v, blks[:, 96:96 + nblk], blks[:, 192:192 + nblk], blks[:, 0:nblk])
                v.tensor_tensor(out=intt[:, 0:1024].rearrange("p (b k) -> p b k", k=32),
                                in0=scr[:, WR:WR + 1024].rearrange("p (b k) -> p b k", k=32),
                                in1=blks[:, 192:192 + nblk].rearrange("p (b o) -> p b o", o=1)
                                .to_broadcast([128, nblk, 32]), op=ALU.mult)
                v.tensor_tensor(out=scrb[:, 3072 * 2:3072 * 2 + 1024].rearrange("p (b k) -> p b k", k=32),
                                in0=intt[:, 0:1024].rearrange("p (b k) -> p b k", k=32),
                                in1=blks[:, 96:96 + nblk].rearrange("p (b o) -> p b o", o=1)
                                .to_broadcast([128, nblk, 32]), op=ALU.mult)
                p4q[idx] = sV.n

            @block.tensor
            def _(t, idx=idx, WQT=WQT):
                t.wait_ge(sV.h, p4q[idx])
                if idx - 1 in p4ev:
                    t.wait_ge(sS.h, p4ev[idx - 1])
                for ic in range(8):
                    t.transpose(pb[6 + ic // 4][:, (ic % 4) * 64:(ic % 4) * 64 + 64].bitcast(BF16),
                                scrb[:, 3072 * 2 + ic * 128:3072 * 2 + (ic + 1) * 128], identb[:])
                p4tr[idx] = sT.n

            @block.scalar
            def _(s, idx=idx, WQT=WQT):
                s.wait_ge(sT.h, p4tr[idx])
                if idx - 2 in p4mm4:
                    s.wait_ge(sT.h, p4mm4[idx - 2])
                for ic in range(8):
                    s.copy(out=scrb[:, WQT * 2 + ic * 128:WQT * 2 + (ic + 1) * 128],
                           in_=pb[6 + ic // 4][:, (ic % 4) * 64:(ic % 4) * 64 + 64].bitcast(BF16))
                p4ev[idx] = sS.n

            for tb in range(4):
                @block.tensor
                def _(t, tb=tb, idx=idx, WQT=WQT):
                    t.wait_ge(sS.h, p4ev[idx])
                    if idx == 0 and tb == 0:
                        waits(t, sS, sV)
                    key = (idx, tb)
                    if (idx - 1, tb) in p4mmev:
                        t.wait_ge(sS.h, p4mmev[(idx - 1, tb)][0])
                        t.wait_ge(sV.h, p4mmev[(idx - 1, tb)][1])
                    for ic in range(8):
                        t.matmul(pb[tb][:], scrb[:, WQT * 2 + ic * 128:WQT * 2 + (ic + 1) * 128],
                                 xqbf[:, ic * T + tb * 512: ic * T + tb * 512 + 512],
                                 start=(ic == 0), stop=(ic == 7))
                    p4mm[(idx, tb)] = sT.n

                @block.scalar
                def _(s, tb=tb, idx=idx):
                    s.wait_ge(sT.h, p4mm[(idx, tb)])
                    prev = (idx, tb - 2) if tb >= 2 else (idx - 1, tb + 2)
                    if prev in p4st:
                        s.wait_ge(sSp[tb % 2].h, p4st[prev])
                    s.copy(out=[stage3, sq_sb][tb % 2][:], in_=pb[tb][:])

                @block.vector
                def _(v, tb=tb, idx=idx, mcol=mcol):
                    v.wait_ge(sT.h, p4mm[(idx, tb)])
                    v.tensor_reduce(out=stats[:, 18:19], in_=pb[tb][:], axis=AX.X,
                                    op=ALU.max, apply_absolute_value=True)
                    v.tensor_max(stats[:, mcol:mcol + 1], stats[:, mcol:mcol + 1], stats[:, 18:19])
                p4mmev[(idx, tb)] = (sS.n, sV.n)

                @block.sync
                def _(sy, ddst=ddst, oc=oc, tb=tb, idx=idx):
                    sy.wait_ge(sS.h, p4mmev[(idx, tb)][0])
                    p4st[(idx, tb)] = pstdma(sy, tb % 2, ddst[oc, :, tb * 512:(tb + 1) * 512],
                                             [stage3, sq_sb][tb % 2][:])
            p4mm4[idx] = p4mm[(idx, 3)]

        if DEBUG_STOP == "p4":
            emit_dummy_output()
            return nc

        part_absmax(5, 1)
        part_absmax(6, 2)
        allreduce_max()
        bcast_cc(1, 0)
        bcast_cc(2, 1)

        @block.vector
        def _(v):
            waits(v, sT)
            scale_cols(v, pb[0][:, 0:1], 20, 21, 32767.0)
            scale_cols(v, pb[1][:, 0:1], 22, 23, 32767.0)
            sV.inc(v.tensor_scalar_mul(stats[:, 24:25], stats[:, 22:23], 1.0 / 65536.0))

        if DEBUG_STOP == "cc2":
            emit_dummy_output()
            return nc

        stage_i = stage[:].bitcast(I32)
        a_ld, a_v = {}, {}
        for oc in range(24):
            @block.sync
            def _(sy, oc=oc):
                if oc == 0:
                    waits(sy, sV, sS)
                    drain_parity(sy)
                if oc - 2 in a_v:
                    sy.wait_ge(sV.h, a_v[oc - 2])
                a_ld[oc] = pdma(sy, oc % 2, [scr[:, 0:2048], scr[:, 6144:8192]][oc % 2], d_gate[oc])

            for hh in range(2):
                H = hh * 1024
                GOF = 0 if oc % 2 == 0 else 6144

                @block.vector
                def _(v, H=H, GOF=GOF, oc=oc):
                    v.wait_ge(sDp[oc % 2].h, a_ld[oc])
                    waits(v, sS, sSt)
                    v.tensor_scalar(out=intt[:], in0=scr[:, GOF + H:GOF + H + 1024],
                                    scalar1=stats[:, 23:24], scalar2=None, op0=ALU.mult)
                    sV.inc(v.nop())

                @block.scalar
                def _(s, H=H):
                    waits(s, sV)
                    sS.inc(s.activation(out=scr[:, 2048 + H:2048 + H + 1024], in_=intt[:],
                                        func=AF.Sigmoid, scale=stats[:, 22:23]))

                @block.vector
                def _(v, H=H):
                    waits(v, sS)
                    v.tensor_scalar_mul(stage_i, scr[:, 2048 + H:2048 + H + 1024], 65536.0)
                    v.tensor_scalar_min(stage_i, stage_i, 65535.0)
                    v.tensor_mul(scr[:, 4096 + H:4096 + H + 1024], intt[:], stage_i)
                    v.tensor_scalar(out=scr[:, 4096 + H:4096 + H + 1024],
                                    in0=scr[:, 4096 + H:4096 + H + 1024],
                                    scalar1=stats[:, 24:25], scalar2=None, op0=ALU.mult)
                    v.tensor_reduce(out=stats[:, 18:19], in_=scr[:, 4096 + H:4096 + H + 1024],
                                    axis=AX.X, op=ALU.max, apply_absolute_value=True)
                    sV.inc(v.tensor_max(stats[:, 7:8], stats[:, 7:8], stats[:, 18:19]))

            @block.sync
            def _(sy, oc=oc):
                waits(sy, sV)
                stdma(sy, d_g2[oc], scr[:, 4096:6144])
                a_v[oc] = sV.n

        part_absmax(7, 3)
        allreduce_max()
        bcast_cc(3, 0)

        @block.vector
        def _(v):
            waits(v, sT)
            scale_cols(v, pb[0][:, 0:1], 26, 27, 32767.0)
            sV.inc(v.tensor_mul(stats[:, 28:29], stats[:, 26:27], stats[:, 20:21]))

        if DEBUG_STOP == "p5a":
            emit_dummy_output()
            return nc

        b5_ld, b5_v = {}, {}
        for oc in range(24):
            @block.sync
            def _(sy, oc=oc):
                if oc == 0:
                    waits(sy, sV)
                    drain_parity(sy)
                if oc - 2 in b5_v:
                    sy.wait_ge(sV.h, b5_v[oc - 2])
                IN = [0, 6144][oc % 2]
                pdma(sy, oc % 2, scr[:, IN:IN + 2048], d_g2[oc])
                b5_ld[oc] = pdma(sy, oc % 2, scr[:, 2048:4096] if oc % 2 == 0 else scr[:, 8192:10240], d_up[oc])

            for hh in range(2):
                H = hh * 1024

                @block.vector
                def _(v, H=H, oc=oc):
                    v.wait_ge(sDp[oc % 2].h, b5_ld[oc])
                    waits(v, sSt)
                    GOF = [0, 6144][oc % 2]
                    UOF = [2048, 8192][oc % 2]
                    v.tensor_scalar(out=intt[:], in0=scr[:, GOF + H:GOF + H + 1024],
                                    scalar1=stats[:, 27:28], scalar2=None, op0=ALU.mult)
                    v.tensor_scalar(out=stage_i, in0=scr[:, UOF + H:UOF + H + 1024],
                                    scalar1=stats[:, 21:22], scalar2=None, op0=ALU.mult)
                    v.tensor_mul(scr[:, 4096 + H:4096 + H + 1024], intt[:], stage_i)
                    v.tensor_scalar(out=scr[:, 4096 + H:4096 + H + 1024],
                                    in0=scr[:, 4096 + H:4096 + H + 1024],
                                    scalar1=stats[:, 28:29], scalar2=None, op0=ALU.mult)
                    v.tensor_reduce(out=stats[:, 18:19], in_=scr[:, 4096 + H:4096 + H + 1024],
                                    axis=AX.X, op=ALU.max, apply_absolute_value=True)
                    sV.inc(v.tensor_max(stats[:, 25:26], stats[:, 25:26], stats[:, 18:19]))

            @block.sync
            def _(sy, oc=oc):
                waits(sy, sV)
                stdma(sy, d_o[oc], scr[:, 4096:6144])
                b5_v[oc] = sV.n

        part_absmax(25, 4)
        allreduce_max()
        bcast_cc(4, 0)

        @block.vector
        def _(v):
            waits(v, sT)
            scale_cols(v, pb[0][:, 0:1], 29, 30, 32767.0)
            sV.inc(v.nop())

        if DEBUG_STOP == "p5b":
            emit_dummy_output()
            return nc

        def wdT_slice(dm, ic):
            if dm < 5:
                return xnbf[:, dm * 3072 + ic * 128: dm * 3072 + (ic + 1) * 128]
            return xqbf[:, (dm - 5) * 3072 + ic * 128: (dm - 5) * 3072 + (ic + 1) * 128]

        for dm in range(8):
            @block.sync
            def _(sy, dm=dm):
                waits(sy, sV, sS, sT)
                sdma(sy, scr[:, 0:3072], w["L0_wd"][dm * 128:(dm + 1) * 128, :])

            @block.scalar
            def _(s):
                waits(s, sD)
                sS.inc(s.activation(out=scr[:, 3072:6144], in_=scr[:, 0:3072], func=AF.Abs))

            @block.vector
            def _(v):
                waits(v, sS, sT)
                nblk = 96
                v.tensor_reduce(out=blks[:, 0:nblk].rearrange("p (b o) -> p b o", o=1),
                                in_=scr[:, 3072:6144].rearrange("p (b k) -> p b k", k=32),
                                axis=AX.X, op=ALU.max)
                v.tensor_scalar(out=blks[:, 96:96 + nblk], in0=blks[:, 0:nblk],
                                scalar1=1.0 / 7.0, scalar2=1e-12, op0=ALU.mult, op1=ALU.max)
                recip_newton(v, blks[:, 96:96 + nblk], blks[:, 192:192 + nblk], blks[:, 0:nblk])
                for half in range(4):
                    h0 = half * 768
                    b0 = half * 24
                    v.tensor_tensor(out=intt[:, 0:768].rearrange("p (b k) -> p b k", k=32),
                                    in0=scr[:, h0:h0 + 768].rearrange("p (b k) -> p b k", k=32),
                                    in1=blks[:, 192 + b0:192 + b0 + 24].rearrange("p (b o) -> p b o", o=1)
                                    .to_broadcast([128, 24, 32]), op=ALU.mult)
                    v.tensor_tensor(out=scrb[:, 6144 * 2 + h0:6144 * 2 + h0 + 768].rearrange("p (b k) -> p b k", k=32),
                                    in0=intt[:, 0:768].rearrange("p (b k) -> p b k", k=32),
                                    in1=blks[:, 96 + b0:96 + b0 + 24].rearrange("p (b o) -> p b o", o=1)
                                    .to_broadcast([128, 24, 32]), op=ALU.mult)
                sV.inc(v.nop())

            transpose_evac(24, 6144 * 2, lambda ic, dm=dm: wdT_slice(dm, ic))

        if DEBUG_STOP == "p5c":
            emit_dummy_output()
            return nc

        c_ld, c_v = {}, {}
        for tb in range(4):
            for oc in range(24):
                i = tb * 24 + oc
                SRC = (i % 2) * 512
                IO = (i % 2) * 512

                @block.sync
                def _(sy, oc=oc, tb=tb, SRC=SRC, i=i):
                    if i == 0:
                        waits(sy, sS, sV, sT)
                        drain_parity(sy)
                    if oc == 0 and tb > 0:
                        waits(sy, sT)
                    if i - 2 in c_v:
                        sy.wait_ge(sV.h, c_v[i - 2])
                    c_ld[i] = pdma(sy, i % 2, scr[:, SRC:SRC + 512],
                                   d_o[oc, :, tb * 512:(tb + 1) * 512])

                @block.vector
                def _(v, oc=oc, SRC=SRC, IO=IO, i=i):
                    v.wait_ge(sDp[i % 2].h, c_ld[i])
                    if i == 0:
                        waits(v, sS)
                    v.tensor_scalar(out=intt[:, IO:IO + 512], in0=scr[:, SRC:SRC + 512],
                                    scalar1=stats[:, 30:31], scalar2=None, op0=ALU.mult)
                    v.tensor_scalar(out=scrb[:, 4096 * 2 + oc * 512:4096 * 2 + oc * 512 + 512],
                                    in0=intt[:, IO:IO + 512],
                                    scalar1=stats[:, 29:30], scalar2=None, op0=ALU.mult)
                    c_v[i] = sV.n

            for dm in range(8):
                @block.tensor
                def _(t, dm=dm):
                    waits(t, sS, sV)
                    for ic in range(24):
                        t.matmul(pb[dm % 4][:], wdT_slice(dm, ic),
                                 scrb[:, 4096 * 2 + ic * 512:4096 * 2 + ic * 512 + 512],
                                 start=(ic == 0), stop=(ic == 23))
                    sT.inc(t.nop())

                @block.scalar
                def _(s, dm=dm):
                    waits(s, sT)
                    sS.inc(s.activation(out=sq_sb[:], in_=pb[dm % 4][:], func=AF.Sigmoid))

                @block.vector
                def _(v, dm=dm):
                    waits(v, sS, sSt)
                    sV.inc(v.tensor_mul([stage3, stage4][dm % 2][:], pb[dm % 4][:], sq_sb[:]))

                @block.sync
                def _(sy, tb=tb, dm=dm):
                    waits(sy, sV)
                    stdma(sy, d_g[dm, :, tb * 512:(tb + 1) * 512], [stage3, stage4][dm % 2][:])

        if DEBUG_STOP == "p5d":
            emit_dummy_output()
            return nc

        for tb in range(4):
            @block.sync
            def _(sy, tb=tb):
                waits(sy, sS, sV, sT)
                sdma(sy, scrb[:, 0:16 * 512].rearrange("p (a f) -> p a f", a=16),
                     AP(d_ssm.tensor, tb * 512, [[T, 128], [128 * T, 16], [1, 512]]))
                for i in range(4):
                    sdma(sy, scr[:, 4096 + i * 1024:4096 + (i + 1) * 1024],
                         dx[tb * 512 + i * 128: tb * 512 + (i + 1) * 128, :])

            for dm in range(8):
                @block.gpsimd
                def _(g, dm=dm):
                    waits(g, sT)
                    gdma(g, scrb[:, 9216 * 2:9216 * 2 + 2048].rearrange("p (c f) -> p c f", c=16),
                         AP(w["L0_Wout"].tensor, dm * 128, [[D, 128], [128 * D, 16], [1, 128]]))

                @block.sync
                def _(sy, tb=tb, dm=dm):
                    waits(sy, sV)
                    sdma(sy, scr[:, 8192 + (dm % 2) * 512:8192 + (dm % 2) * 512 + 512],
                         d_g[dm, :, tb * 512:(tb + 1) * 512])
                    sdma(sy, stage3[:].rearrange("p (a f) -> p a f", a=4),
                         AP(d_xn.tensor, dm * 128 * 128 + (tb * 4) * 8 * 128 * 128,
                            [[128, 128], [8 * 128 * 128, 4], [1, 128]]))

                @block.tensor
                def _(t, tb=tb, dm=dm):
                    waits(t, sG, sD, sV)
                    bk = dm % 2
                    wbase = 9216 * 2
                    for sc2 in range(16):
                        t.matmul(pb[bk][:], scrb[:, wbase + sc2 * 128:wbase + (sc2 + 1) * 128],
                                 scrb[:, sc2 * 512:(sc2 + 1) * 512],
                                 start=(sc2 == 0), stop=(sc2 == 15))
                    for i in range(4):
                        t.transpose(pb[4 + bk][:, i * 128:(i + 1) * 128],
                                    scr[:, 4096 + i * 1024 + dm * 128:4096 + i * 1024 + (dm + 1) * 128],
                                    ident[:])
                    sT.inc(t.nop())

                @block.vector
                def _(v, tb=tb, dm=dm):
                    waits(v, sT, sD)
                    bk = dm % 2
                    d0 = dm * T + tb * 512
                    v.tensor_add(x1[:, d0:d0 + 512], pb[bk][:], stage3[:])
                    v.tensor_mul(x1[:, d0:d0 + 512], x1[:, d0:d0 + 512],
                                 scr[:, 8192 + bk * 512:8192 + bk * 512 + 512])
                    sV.inc(v.tensor_add(x1[:, d0:d0 + 512], x1[:, d0:d0 + 512], pb[4 + bk][:]))

        def emit_output(src):
            for tb in range(4):
                for dm in range(8):
                    @block.tensor
                    def _(t, tb=tb, dm=dm):
                        waits(t, sV, sS)
                        for i in range(4):
                            t.transpose(pb[6][:, i * 128:(i + 1) * 128],
                                        src[:, dm * T + tb * 512 + i * 128: dm * T + tb * 512 + (i + 1) * 128],
                                        ident[:])
                        sT.inc(t.nop())

                    @block.scalar
                    def _(s, tb=tb, dm=dm):
                        waits(s, sT, sSt)
                        for i in range(4):
                            s.copy(out=scr[:, i * 1024 + dm * 128: i * 1024 + (dm + 1) * 128],
                                   in_=pb[6][:, i * 128:(i + 1) * 128])
                        sS.inc(s.nop())

                @block.sync
                def _(sy, tb=tb):
                    waits(sy, sS)
                    for i in range(4):
                        stdma(sy, dout[tb * 512 + i * 128: tb * 512 + (i + 1) * 128, :],
                              scr[:, i * 1024:(i + 1) * 1024])

        if DEBUG_STOP == "l0":
            emit_output(x1)

            @block.sync
            def _(sy):
                waits(sy, sSt)
            return nc

        if DEBUG_STOP == "p6pre":
            emit_dummy_output()
            return nc

        for tb in range(4):
            for c in range(8):
                @block.scalar
                def _(s, c=c, tb=tb):
                    waits(s, sV, sT)
                    sS.inc(s.activation(out=sq_sb[:], in_=x1[:, c * T + tb * 512:c * T + tb * 512 + 512],
                                        func=AF.Square))

                @block.tensor
                def _(t, c=c, tb=tb):
                    waits(t, sS)
                    t.matmul(pb[1][0:1, :], ones_c[:], sq_sb[:], start=(c == 0), stop=(c == 7))
                    t.matmul(pb[0][0:1, :], ones_c[:], x1[:, c * T + tb * 512:c * T + tb * 512 + 512],
                             start=(c == 0), stop=(c == 7))
                    sT.inc(t.nop())

            @block.vector
            def _(v):
                waits(v, sT)
                v.tensor_scalar_mul(rowb[0:1, 0:512], pb[0][0:1, :], 1.0 / D)
                v.tensor_mul(rowa[0:1, 0:512], rowb[0:1, 0:512], rowb[0:1, 0:512])
                v.tensor_scalar_mul(rowa[0:1, 0:512], rowa[0:1, 0:512], -1.0)
                v.scalar_tensor_tensor(out=rowa[0:1, 0:512], in0=pb[1][0:1, :], scalar=1.0 / D,
                                       in1=rowa[0:1, 0:512], op0=ALU.mult, op1=ALU.add)
                sV.inc(v.tensor_scalar_add(rowa[0:1, 0:512], rowa[0:1, 0:512], 1e-5))

            @block.scalar
            def _(s):
                waits(s, sV)
                sS.inc(s.activation(out=rowa[0:1, 0:512], in_=rowa[0:1, 0:512], func=AF.Sqrt))

            @block.vector
            def _(v, tb=tb):
                waits(v, sS)
                recip_newton(v, rowa[0:1, 0:512], rowa[0:1, 512:1024],
                             rowq[0:1, tb * 512:tb * 512 + 512])
                sV.inc(v.tensor_copy(out=rowa[0:1, 0:512], in_=rowa[0:1, 512:1024]))

            @block.vector
            def _(v):
                waits(v, sS)
                v.tensor_mul(rowb[0:1, 0:512], rowb[0:1, 0:512], rowa[0:1, 0:512])
                sV.inc(v.tensor_scalar_mul(rowb[0:1, 0:512], rowb[0:1, 0:512], -1.0))

            @block.tensor
            def _(t):
                waits(t, sV)
                t.matmul(pb[2][:], ones_r[:], rowa[0:1, 0:512], start=True, stop=True)
                t.matmul(pb[3][:], ones_r[:], rowb[0:1, 0:512], start=True, stop=True)
                sT.inc(t.nop())

            @block.vector
            def _(v, tb=tb):
                waits(v, sT)
                for c in range(8):
                    v.scalar_tensor_tensor(out=scr[:, c * 512:(c + 1) * 512],
                                           in0=x1[:, c * T + tb * 512:c * T + tb * 512 + 512],
                                           scalar=1.0, in1=pb[2][:], op0=ALU.mult, op1=ALU.mult)
                    v.tensor_add(scr[:, c * 512:(c + 1) * 512], scr[:, c * 512:(c + 1) * 512], pb[3][:])
                    v.tensor_copy(out=xnbf[:, c * T + tb * 512:c * T + tb * 512 + 512],
                                  in_=scr[:, c * 512:(c + 1) * 512])
                v.scalar_tensor_tensor(out=scr[:, 4096:4608], in0=scr[:, 0:512], scalar=-1.0,
                                       in1=scr[:, 0:512], op0=ALU.mult, op1=ALU.max)
                for c in range(1, 8):
                    v.scalar_tensor_tensor(out=scr[:, 4608:5120], in0=scr[:, c * 512:(c + 1) * 512],
                                           scalar=-1.0, in1=scr[:, c * 512:(c + 1) * 512],
                                           op0=ALU.mult, op1=ALU.max)
                    v.tensor_max(scr[:, 4096:4608], scr[:, 4096:4608], scr[:, 4608:5120])
                sV.inc(v.nop())

            @block.tensor
            def _(t):
                waits(t, sV)
                for i in range(4):
                    t.transpose(pb[4][:, i * 128:(i + 1) * 128],
                                scr[:, 4096 + i * 128:4096 + (i + 1) * 128], ident[:])
                sT.inc(t.nop())

            @block.vector
            def _(v):
                waits(v, sT)
                for i in range(4):
                    v.tensor_reduce(out=stats[:, 32 + i:33 + i], in_=pb[4][:, i * 128:(i + 1) * 128],
                                    axis=AX.X, op=ALU.max)
                sV.inc(v.nop())

            @block.tensor
            def _(t):
                waits(t, sV)
                for i in range(4):
                    t.transpose(pb[5][0:1, i * 128:(i + 1) * 128], stats[:, 32 + i:33 + i], ident[:])
                sT.inc(t.nop())

            @block.vector
            def _(v, tb=tb):
                waits(v, sT)
                v.tensor_scalar_max(rowa[0:1, 0:512], pb[5][0:1, 0:512], 1e-5)
                v.tensor_scalar_mul(rowq[0:1, tb * 512:(tb + 1) * 512], rowa[0:1, 0:512], 1.0 / 127.0)
                recip_newton(v, rowa[0:1, 0:512], rowb[0:1, 0:512], rowa[0:1, 512:1024])
                sV.inc(v.tensor_scalar_mul(rowb[0:1, 0:512], rowb[0:1, 0:512], 127.0))

            @block.tensor
            def _(t):
                waits(t, sV)
                t.matmul(pb[6][:], ones_r[:], rowb[0:1, 0:512], start=True, stop=True)
                sT.inc(t.nop())

            for c in range(8):
                @block.vector
                def _(v, c=c):
                    waits(v, sT, sS)
                    sV.inc(v.tensor_tensor(out=intt[:, (c % 2) * 512:(c % 2) * 512 + 512],
                                           in0=scr[:, c * 512:(c + 1) * 512], in1=pb[6][:], op=ALU.mult))

                @block.scalar
                def _(s, c=c, tb=tb):
                    waits(s, sV)
                    sS.inc(s.copy(out=xqbf[:, c * T + tb * 512:c * T + tb * 512 + 512],
                                  in_=intt[:, (c % 2) * 512:(c % 2) * 512 + 512]))

        if DEBUG_STOP == "p7":
            emit_dummy_output()
            return nc

        for wi, (wname, chunks, nI) in enumerate([("L1_wg", 32, 1024), ("L1_wu", 32, 1024), ("L1_wd", 8, 4096)]):
            acc = 46 + wi
            for oc in range(chunks):
                @block.sync
                def _(sy, wname=wname, oc=oc, nI=nI):
                    waits(sy, sS, sV)
                    sdma(sy, scr[:, 0:nI], w[wname][oc * 128:(oc + 1) * 128, :])

                @block.scalar
                def _(s, nI=nI):
                    waits(s, sD, sV)
                    sS.inc(s.activation(out=scr[:, 4096:4096 + nI], in_=scr[:, 0:nI], func=AF.Abs))

                @block.vector
                def _(v, nI=nI, acc=acc, oc=oc):
                    waits(v, sS)
                    v.tensor_reduce(out=stats[:, 36:37], in_=scr[:, 4096:4096 + nI], axis=AX.X, op=ALU.add)
                    if oc == 0:
                        sV.inc(v.tensor_copy(out=stats[:, acc:acc + 1], in_=stats[:, 36:37]))
                    else:
                        sV.inc(v.tensor_add(stats[:, acc:acc + 1], stats[:, acc:acc + 1], stats[:, 36:37]))

            @block.tensor
            def _(t, acc=acc):
                waits(t, sV)
                t.transpose(pb[0][0:1, 0:128], stats[:, acc:acc + 1], ident[:])
                sT.inc(t.nop())

            @block.vector
            def _(v, wi=wi, chunks=chunks, nI=nI):
                waits(v, sT)
                v.tensor_reduce(out=rowc[0:1, wi:wi + 1], in_=pb[0][0:1, 0:128], axis=AX.X, op=ALU.add)
                sV.inc(v.tensor_scalar(out=rowc[0:1, 8 + wi:9 + wi], in0=rowc[0:1, wi:wi + 1],
                                       scalar1=1.0 / (chunks * 128 * nI), scalar2=1e-5,
                                       op0=ALU.mult, op1=ALU.max))

        @block.tensor
        def _(t):
            waits(t, sV)
            for wi in range(3):
                t.matmul(pb[wi][:, 0:1], ones_r[:], rowc[0:1, 8 + wi:9 + wi], start=True, stop=True)
            sT.inc(t.nop())

        @block.vector
        def _(v):
            waits(v, sT)
            for wi in range(3):
                v.tensor_copy(out=stats[:, 41 + 2 * wi:42 + 2 * wi], in_=pb[wi][:, 0:1])
                recip_newton(v, stats[:, 41 + 2 * wi:42 + 2 * wi], stats[:, 40 + 2 * wi:41 + 2 * wi],
                             stats[:, 19:20])
            sV.inc(v.nop())

        b_ld, b_q, b_tn, b_tr, b_te, b_mm, b_ev, b_st, b_mm4 = {}, {}, {}, {}, {}, {}, {}, {}, {}
        for wi, (wname, ddst) in enumerate([("L1_wg", d_gate1), ("L1_wu", d_up1)]):
            scol = 40 + 2 * wi
            invcol = 41 + 2 * wi
            for tb in range(4):
                @block.vector
                def _(v, invcol=invcol, tb=tb):
                    waits(v, sS, sT)
                    sV.inc(v.tensor_scalar(out=rowa[0:1, 0:512], in0=rowq[0:1, tb * 512:(tb + 1) * 512],
                                           scalar1=stats[0:1, invcol:invcol + 1], scalar2=None, op0=ALU.mult))

                @block.tensor
                def _(t, tb=tb):
                    waits(t, sV, sS)
                    t.matmul(pb[6][:], ones_r[:], rowa[0:1, 0:512], start=True, stop=True)
                    sT.inc(t.nop())

                @block.vector
                def _(v, tb=tb):
                    waits(v, sT)
                    sV.inc(v.tensor_copy(out=scr[:, 4096 + tb * 512:4096 + (tb + 1) * 512], in_=pb[6][:]))

            for oc in range(32):
                idx = wi * 32 + oc
                WR = (idx % 2) * 1024
                TN = 2048 + (idx % 2) * 512
                WQT = 3072 + (idx % 2) * 512

                @block.sync
                def _(sy, wname=wname, oc=oc, WR=WR, idx=idx):
                    if idx == 0:
                        waits(sy, sV, sS, sT)
                        drain_parity(sy)
                    if idx - 2 in b_q:
                        sy.wait_ge(sV.h, b_q[idx - 2])
                    b_ld[idx] = pdma(sy, idx % 2, scr[:, WR:WR + 1024], w[wname][oc * 128:(oc + 1) * 128, :])

                @block.vector
                def _(v, scol=scol, WR=WR, idx=idx):
                    v.wait_ge(sDp[idx % 2].h, b_ld[idx])
                    if idx - 1 in b_tn:
                        v.wait_ge(sS.h, b_tn[idx - 1])
                    v.tensor_scalar(out=intt[:, 0:1024], in0=scr[:, WR:WR + 1024],
                                    scalar1=stats[:, scol:scol + 1], scalar2=1.0, op0=ALU.mult, op1=ALU.min)
                    v.tensor_scalar_max(intt[:, 0:1024], intt[:, 0:1024], -1.0)
                    b_q[idx] = sV.n

                @block.scalar
                def _(s, TN=TN, idx=idx):
                    s.wait_ge(sV.h, b_q[idx])
                    if idx - 2 in b_tr:
                        s.wait_ge(sT.h, b_tr[idx - 2])
                    s.copy(out=scrb[:, TN * 2:TN * 2 + 1024], in_=intt[:, 0:1024])
                    b_tn[idx] = sS.n

                @block.tensor
                def _(t, TN=TN, idx=idx):
                    t.wait_ge(sS.h, b_tn[idx])
                    if idx - 1 in b_te:
                        t.wait_ge(sS.h, b_te[idx - 1])
                    for ic in range(8):
                        t.transpose(pb[6 + ic // 4][:, (ic % 4) * 64:(ic % 4) * 64 + 64].bitcast(BF16),
                                    scrb[:, TN * 2 + ic * 128:TN * 2 + (ic + 1) * 128], identb[:])
                    b_tr[idx] = sT.n

                @block.scalar
                def _(s, WQT=WQT, idx=idx):
                    s.wait_ge(sT.h, b_tr[idx])
                    if idx - 2 in b_mm4:
                        s.wait_ge(sT.h, b_mm4[idx - 2])
                    for ic in range(8):
                        s.copy(out=scrb[:, WQT * 2 + ic * 128:WQT * 2 + (ic + 1) * 128],
                               in_=pb[6 + ic // 4][:, (ic % 4) * 64:(ic % 4) * 64 + 64].bitcast(BF16))
                    b_te[idx] = sS.n

                for tb in range(4):
                    @block.tensor
                    def _(t, tb=tb, idx=idx, WQT=WQT):
                        t.wait_ge(sS.h, b_te[idx])
                        if (idx - 1, tb) in b_ev:
                            t.wait_ge(sV.h, b_ev[(idx - 1, tb)])
                        for ic in range(8):
                            t.matmul(pb[tb % 2][:], scrb[:, WQT * 2 + ic * 128:WQT * 2 + (ic + 1) * 128],
                                     xqbf[:, ic * T + tb * 512: ic * T + tb * 512 + 512],
                                     start=(ic == 0), stop=(ic == 7))
                        b_mm[(idx, tb)] = sT.n

                    @block.vector
                    def _(v, tb=tb, idx=idx):
                        v.wait_ge(sT.h, b_mm[(idx, tb)])
                        prev = (idx, tb - 2) if tb >= 2 else (idx - 1, tb + 2)
                        if prev in b_st:
                            v.wait_ge(sSp[tb % 2].h, b_st[prev])
                        v.tensor_mul([stage3, sq_sb][tb % 2][:], pb[tb % 2][:],
                                     scr[:, 4096 + tb * 512:4096 + (tb + 1) * 512])
                        b_ev[(idx, tb)] = sV.n

                    @block.sync
                    def _(sy, ddst=ddst, oc=oc, tb=tb, idx=idx):
                        sy.wait_ge(sV.h, b_ev[(idx, tb)])
                        b_st[(idx, tb)] = pstdma(sy, tb % 2, ddst[oc, :, tb * 512:(tb + 1) * 512],
                                                 [stage3, sq_sb][tb % 2][:])
                b_mm4[idx] = b_mm[(idx, 3)]

        for dm in range(8):
            @block.sync
            def _(sy, dm=dm):
                waits(sy, sS, sV, sT, sSt)
                sdma(sy, scr[:, 0:4096], w["L1_wd"][dm * 128:(dm + 1) * 128, :])

            @block.vector
            def _(v):
                waits(v, sD, sS)
                for half in range(4):
                    h0 = half * 1024
                    v.tensor_scalar(out=intt[:, 0:1024], in0=scr[:, h0:h0 + 1024],
                                    scalar1=stats[:, 44:45], scalar2=1.0, op0=ALU.mult, op1=ALU.min)
                    v.tensor_scalar_max(intt[:, 0:1024], intt[:, 0:1024], -1.0)
                    v.tensor_copy(out=scrb[:, 8192 * 2 + h0:8192 * 2 + h0 + 1024], in_=intt[:, 0:1024])
                sV.inc(v.nop())

            for half in range(2):
                transpose_evac(16, 8192 * 2 + half * 2048,
                               lambda ic: scrb[:, 6144 * 2 + ic * 128:6144 * 2 + (ic + 1) * 128])

                @block.sync
                def _(sy, dm=dm, half=half):
                    waits(sy, sS)
                    stdma(sy, d_wdT1[dm, :, half * 2048:(half + 1) * 2048],
                          scrb[:, 6144 * 2:6144 * 2 + 2048])

        stageb_i = stage
        for tb in range(4):
            dmk, smk, vmk = {}, {}, {}
            for c0 in range(0, 32, 2):
                @block.sync
                def _(sy, c0=c0, tb=tb):
                    if c0 == 0:
                        waits(sy, sV, sS)
                        drain_parity(sy)
                    for i in (c0, c0 + 1):
                        if i - 2 in vmk:
                            sy.wait_ge(sV.h, vmk[i - 2])
                        pdma(sy, i % 2, scr[:, (i % 2) * 512:(i % 2) * 512 + 512],
                             d_gate1[i, :, tb * 512:(tb + 1) * 512])
                        dmk[i] = pdma(sy, i % 2, scr[:, 1024 + (i % 2) * 512:1024 + (i % 2) * 512 + 512],
                                      d_up1[i, :, tb * 512:(tb + 1) * 512])

                @block.scalar
                def _(s, c0=c0):
                    for i in (c0, c0 + 1):
                        s.wait_ge(sDp[i % 2].h, dmk[i])
                        if i - 2 in vmk:
                            s.wait_ge(sV.h, vmk[i - 2])
                        s.activation(out=stage[:, (i % 2) * 512:(i % 2) * 512 + 512],
                                     in_=scr[:, (i % 2) * 512:(i % 2) * 512 + 512], func=AF.Sigmoid)
                        smk[i] = sS.n

                @block.vector
                def _(v, c0=c0):
                    for i in (c0, c0 + 1):
                        v.wait_ge(sS.h, smk[i])
                        v.tensor_mul(stage3[:], scr[:, (i % 2) * 512:(i % 2) * 512 + 512],
                                     stage[:, (i % 2) * 512:(i % 2) * 512 + 512])
                        v.tensor_mul(stage3[:], stage3[:],
                                     scr[:, 1024 + (i % 2) * 512:1024 + (i % 2) * 512 + 512])
                        v.scalar_tensor_tensor(out=stage3[:], in0=stage3[:], scalar=-1.0,
                                               in1=stage3[:], op0=ALU.mult, op1=ALU.max)
                        if i == 0:
                            v.tensor_copy(out=scr[:, 2048:2560], in_=stage3[:])
                        else:
                            v.tensor_max(scr[:, 2048:2560], scr[:, 2048:2560], stage3[:])
                        vmk[i] = sV.n

            @block.tensor
            def _(t):
                waits(t, sV)
                for i in range(4):
                    t.transpose(pb[4][:, i * 128:(i + 1) * 128],
                                scr[:, 2048 + i * 128:2048 + (i + 1) * 128], ident[:])
                sT.inc(t.nop())

            @block.vector
            def _(v):
                waits(v, sT)
                for i in range(4):
                    v.tensor_reduce(out=stats[:, 32 + i:33 + i], in_=pb[4][:, i * 128:(i + 1) * 128],
                                    axis=AX.X, op=ALU.max)
                sV.inc(v.nop())

            @block.tensor
            def _(t):
                waits(t, sV)
                for i in range(4):
                    t.transpose(pb[5][0:1, i * 128:(i + 1) * 128], stats[:, 32 + i:33 + i], ident[:])
                sT.inc(t.nop())

            @block.vector
            def _(v):
                waits(v, sT)
                v.tensor_scalar_max(rowa[0:1, 0:512], pb[5][0:1, 0:512], 1e-5)
                v.tensor_scalar(out=rowa[0:1, 512:1024], in0=rowa[0:1, 0:512],
                                scalar1=1.0 / 127.0, scalar2=None, op0=ALU.mult)
                recip_newton(v, rowa[0:1, 0:512], rowb[0:1, 0:512], rowb[0:1, 512:1024])
                v.tensor_scalar_mul(rowb[0:1, 0:512], rowb[0:1, 0:512], 127.0)
                sV.inc(v.tensor_scalar(out=rowa[0:1, 512:1024], in0=rowa[0:1, 512:1024],
                                       scalar1=stats[0:1, 45:46], scalar2=None, op0=ALU.mult))

            @block.tensor
            def _(t):
                waits(t, sV)
                t.matmul(pb[6][:], ones_r[:], rowb[0:1, 0:512], start=True, stop=True)
                t.matmul(pb[7][:], ones_r[:], rowa[0:1, 512:1024], start=True, stop=True)
                sT.inc(t.nop())

            dmk, smk, vmk = {}, {}, {}
            for c0 in range(0, 32, 2):
                @block.sync
                def _(sy, c0=c0, tb=tb):
                    if c0 == 0:
                        waits(sy, sV, sS, sT)
                        drain_parity(sy)
                    for i in (c0, c0 + 1):
                        if i - 2 in vmk:
                            sy.wait_ge(sV.h, vmk[i - 2])
                        pdma(sy, i % 2, scr[:, (i % 2) * 512:(i % 2) * 512 + 512],
                             d_gate1[i, :, tb * 512:(tb + 1) * 512])
                        dmk[i] = pdma(sy, i % 2, scr[:, 1024 + (i % 2) * 512:1024 + (i % 2) * 512 + 512],
                                      d_up1[i, :, tb * 512:(tb + 1) * 512])

                @block.scalar
                def _(s, c0=c0):
                    for i in (c0, c0 + 1):
                        s.wait_ge(sDp[i % 2].h, dmk[i])
                        if i - 2 in vmk:
                            s.wait_ge(sV.h, vmk[i - 2])
                        s.activation(out=stage[:, (i % 2) * 512:(i % 2) * 512 + 512],
                                     in_=scr[:, (i % 2) * 512:(i % 2) * 512 + 512], func=AF.Sigmoid)
                        smk[i] = sS.n

                @block.vector
                def _(v, c0=c0):
                    for i in (c0, c0 + 1):
                        v.wait_ge(sS.h, smk[i])
                        v.tensor_mul(stage3[:], scr[:, (i % 2) * 512:(i % 2) * 512 + 512],
                                     stage[:, (i % 2) * 512:(i % 2) * 512 + 512])
                        v.tensor_mul(stage3[:], stage3[:],
                                     scr[:, 1024 + (i % 2) * 512:1024 + (i % 2) * 512 + 512])
                        v.tensor_tensor(out=intt[:, (i % 2) * 512:(i % 2) * 512 + 512],
                                        in0=stage3[:], in1=pb[6][:], op=ALU.mult)
                        v.tensor_copy(out=scrb[:, 2048 * 2 + i * 512:2048 * 2 + i * 512 + 512],
                                      in_=intt[:, (i % 2) * 512:(i % 2) * 512 + 512])
                        vmk[i] = sV.n

            for dm in range(8):
                @block.sync
                def _(sy, dm=dm):
                    waits(sy, sT)
                    if dm == 0:
                        waits(sy, sV)
                    sdma(sy, stage[:].bitcast(BF16), d_wdT1[dm, :, 0:2048])
                    sdma(sy, stage2[:].bitcast(BF16), d_wdT1[dm, :, 2048:4096])

                @block.tensor
                def _(t, dm=dm):
                    waits(t, sD, sS, sV)
                    for ic in range(32):
                        lhs = stage[:].bitcast(BF16) if ic < 16 else stage2[:].bitcast(BF16)
                        icc = ic % 16
                        t.matmul(pb[dm % 2][:], lhs[:, icc * 128:(icc + 1) * 128],
                                 scrb[:, 2048 * 2 + ic * 512:2048 * 2 + ic * 512 + 512],
                                 start=(ic == 0), stop=(ic == 31))
                    sT.inc(t.nop())

                @block.vector
                def _(v, dm=dm):
                    waits(v, sT, sSt)
                    v.tensor_copy(out=sq_sb[:], in_=pb[7][:])
                    sV.inc(v.tensor_mul(stage3[:], pb[dm % 2][:], sq_sb[:]))

                @block.scalar
                def _(s):
                    waits(s, sV)
                    sS.inc(s.activation(out=sq_sb[:], in_=stage3[:], func=AF.Sigmoid))

                @block.vector
                def _(v, dm=dm):
                    waits(v, sS)
                    sV.inc(v.tensor_mul(stage3[:], stage3[:], sq_sb[:]))

                @block.sync
                def _(sy, tb=tb, dm=dm):
                    waits(sy, sV)
                    stdma(sy, d_g1[dm, :, tb * 512:(tb + 1) * 512], stage3[:])

        QOF, KOF, VOF, VTA, WQKV, SCA, ZOF, EOF = 0, 1024, 2048, 3072, 4112, 5136, 7696, 9232
        for hg in range(8):
            @block.gpsimd
            def _(g, hg=hg):
                waits(g, sT, sS, sV)
                stgb = stage[:].bitcast(BF16)
                for wi, wn in enumerate(["L1_Wq", "L1_Wk", "L1_Wv"]):
                    dst = (stgb[:, 0:1024] if wi == 2 else
                           scrb[:, (WQKV + wi * 512) * 2:(WQKV + wi * 512) * 2 + 1024])
                    gdma(g, dst.rearrange("p (c f) -> p c f", c=8),
                         AP(w[wn].tensor, hg * 128, [[D, 128], [128 * D, 8], [1, 128]]))
                g.memset(scrb[:, VTA * 2:VTA * 2 + 2080], 1.0)
                sGc.inc(g.nop())

            for wi, OF in enumerate([QOF, KOF, VOF]):
                for tb in range(4):
                    @block.tensor
                    def _(t, wi=wi, tb=tb):
                        waits(t, sGc, sG, sS, sV)
                        stgb = stage[:].bitcast(BF16)
                        for ic in range(8):
                            lhs = (stgb[:, ic * 128:(ic + 1) * 128] if wi == 2 else
                                   scrb[:, (WQKV + wi * 512) * 2 + ic * 128:(WQKV + wi * 512) * 2 + (ic + 1) * 128])
                            t.matmul(pb[tb % 2][:], lhs,
                                     xnbf[:, ic * T + tb * 512: ic * T + tb * 512 + 512],
                                     start=(ic == 0), stop=(ic == 7))
                        sT.inc(t.nop())

                    @block.scalar
                    def _(s, OF=OF, tb=tb):
                        waits(s, sT)
                        sS.inc(s.copy(out=scrb[:, OF * 2 + tb * 512:OF * 2 + tb * 512 + 512],
                                      in_=pb[tb % 2][:]))

            for h in range(2):
                for g0 in range(0, 16, 8):
                    @block.tensor
                    def _(t, h=h, g0=g0):
                        waits(t, sS, sGc)
                        for j in range(8):
                            kc = g0 + j
                            t.transpose(pb[2 + j // 4][:, (j % 4) * 32:(j % 4) * 32 + 32].bitcast(BF16),
                                        scrb[:, VOF * 2 + kc * 128:VOF * 2 + (kc + 1) * 128][h * 64:(h + 1) * 64, :],
                                        identb[h * 64:(h + 1) * 64, h * 64:h * 64 + 64])
                        sT.inc(t.nop())

                    @block.scalar
                    def _(s, h=h, g0=g0):
                        waits(s, sT)
                        base = (VTA + h * 520) * 2
                        for j in range(8):
                            kc = g0 + j
                            s.copy(out=scrb[:, base + kc * 65:base + kc * 65 + 64],
                                   in_=pb[2 + j // 4][:, (j % 4) * 32:(j % 4) * 32 + 32].bitcast(BF16))
                        sS.inc(s.nop())

            for h in range(2):
                @block.sync
                def _(sy, hg=hg, h=h):
                    waits(sy, sV, sS)
                    sdma(sy, scr[:, SCA:SCA + SCW], w["L1_sc"][2 * hg + h])

                @block.vector
                def _(v):
                    waits(v, sD)
                    sV.inc(v.tensor_scalar(out=scr[:, SCA:SCA + SCW], in0=scr[:, SCA:SCA + SCW],
                                           scalar1=1.0 / 64.0, scalar2=-EXPC, op0=ALU.mult, op1=ALU.add))

                for qb in range(4):
                    kcmax = min(15, (qb * 512 + 511) // 128)
                    mark_sc, mark_z, mark_e = {}, {}, {}
                    CH = 4
                    st2b = stage2[:].bitcast(BF16)

                    def zsl(kc):
                        return scr[:, ZOF + (kc % 4) * 512:ZOF + (kc % 4) * 512 + 512]

                    def esl(kc):
                        return st2b[:, (kc % 4) * 512:(kc % 4) * 512 + 512]

                    def emit_avs(t, js, h, kcmax):
                        base = (VTA + h * 520) * 2
                        for j in js:
                            t.wait_ge(sS.h, mark_e[j])
                            t.matmul(pb[4][0:65, :], scrb[:, base + j * 65:base + (j + 1) * 65],
                                     esl(j), start=(j == 0), stop=(j == kcmax),
                                     skip_group_check=True)

                    for c0 in range(0, kcmax + 1, CH):
                        cN = min(c0 + CH, kcmax + 1)

                        @block.tensor
                        def _(t, c0=c0, cN=cN, h=h, qb=qb, kcmax=kcmax):
                            if c0 == 0:
                                waits(t, sS, sV)
                            emit_avs(t, range(max(0, c0 - CH), c0), h, kcmax)
                            for kc in range(c0, cN):
                                if kc - 4 in mark_z:
                                    t.wait_ge(sV.h, mark_z[kc - 4])
                                if kc - 4 in mark_e:
                                    t.wait_ge(sS.h, mark_e[kc - 4])
                                t.matmul(pb[kc % 4][:],
                                         scrb[:, KOF * 2 + kc * 128:KOF * 2 + (kc + 1) * 128][h * 64:(h + 1) * 64, :],
                                         scrb[:, QOF * 2 + qb * 512:QOF * 2 + (qb + 1) * 512][h * 64:(h + 1) * 64, :],
                                         start=True, stop=True)
                                mark_sc[kc] = sT.n

                        @block.vector
                        def _(v, c0=c0, cN=cN, h=h, qb=qb):
                            for kc in range(c0, cN):
                                v.wait_ge(sT.h, mark_sc[kc])
                                off = SCA + (qb * 512 - kc * 128) + 511
                                v.scalar_tensor_tensor(
                                    out=zsl(kc), in0=pb[kc % 4][:], scalar=1.0 / 64.0,
                                    in1=scr[:, off:off + 512], op0=ALU.mult, op1=ALU.add)
                                mark_z[kc] = sV.n

                        @block.scalar
                        def _(s, c0=c0, cN=cN):
                            for kc in range(c0, cN):
                                s.wait_ge(sV.h, mark_z[kc])
                                s.activation(out=esl(kc), in_=zsl(kc), func=AF.Exp)
                                mark_e[kc] = sS.n

                    @block.tensor
                    def _(t, h=h, kcmax=kcmax):
                        last0 = (kcmax // CH) * CH
                        emit_avs(t, range(last0, kcmax + 1), h, kcmax)
                        sT.inc(t.nop())

                    @block.vector
                    def _(v):
                        waits(v, sT)
                        recip_newton(v, pb[4][64:65, 0:512], rowa[0:1, 0:512], rowb[0:1, 0:512])
                        sV.inc(v.nop())

                    @block.tensor
                    def _(t):
                        waits(t, sV)
                        t.matmul(pb[5][0:64, :], ones_r[0:1, 0:64], rowa[0:1, 0:512], start=True, stop=True)
                        sT.inc(t.nop())

                    @block.vector
                    def _(v, hg=hg, h=h, qb=qb):
                        waits(v, sT)
                        v.tensor_copy(out=stage3[0:64, :], in_=pb[5][0:64, :])
                        sV.inc(v.tensor_tensor(
                            out=xqbf[h * 64:(h + 1) * 64, hg * T + qb * 512:hg * T + qb * 512 + 512],
                            in0=pb[4][0:64, :], in1=stage3[0:64, :], op=ALU.mult))

        for tb in range(4):
            for dm in range(8):
                @block.gpsimd
                def _(g, dm=dm):
                    waits(g, sT)
                    gdma(g, scrb[:, 8192 + (dm % 2) * 1024:8192 + (dm % 2 + 1) * 1024]
                         .rearrange("p (c f) -> p c f", c=8),
                         AP(w["L1_Wo"].tensor, dm * 128, [[D, 128], [128 * D, 8], [1, 128]]))

                @block.sync
                def _(sy, tb=tb, dm=dm):
                    waits(sy, sV)
                    sdma(sy, scr[:, 5376 + (dm % 2) * 512:5376 + (dm % 2) * 512 + 512],
                         d_g1[dm, :, tb * 512:(tb + 1) * 512])

                @block.tensor
                def _(t, tb=tb, dm=dm):
                    waits(t, sG, sV)
                    for ic in range(8):
                        t.matmul(pb[dm % 2][:],
                                 scrb[:, 8192 + (dm % 2) * 1024 + ic * 128:8192 + (dm % 2) * 1024 + (ic + 1) * 128],
                                 xqbf[:, ic * T + tb * 512:ic * T + tb * 512 + 512],
                                 start=(ic == 0), stop=(ic == 7))
                    sT.inc(t.nop())

                @block.vector
                def _(v, tb=tb, dm=dm):
                    waits(v, sT, sD, sS)
                    d0 = dm * T + tb * 512
                    v.tensor_scalar(out=stage3[:], in0=pb[dm % 2][:], scalar1=bo_sb[:, dm:dm + 1],
                                    scalar2=None, op0=ALU.add)
                    v.tensor_mul(stage3[:], stage3[:], scr[:, 5376 + (dm % 2) * 512:5376 + (dm % 2) * 512 + 512])
                    sV.inc(v.tensor_add(x1[:, d0:d0 + 512], x1[:, d0:d0 + 512], stage3[:]))

        if DEBUG_DUMP is not None:
            if DEBUG_DUMP in ("xnbf", "xqbf"):
                sb_t = {"xnbf": xnbf, "xqbf": xqbf}[DEBUG_DUMP]

                @block.gpsimd
                def _(g):
                    waits(g, sSt, sV, sS, sT, sD)
                    gdma(g, AP(dout.tensor, 0, [[16384, 128], [1, 16384]]), sb_t[:])
                    waits(g, sG)
                return nc
            dt_map = {"d_g1": d_g1, "d_gate1": d_gate1, "d_up1": d_up1, "d_g": d_g,
                      "d_ssm": d_ssm, "d_xn": d_xn, "d_wdT1": d_wdT1}
            src_t = dt_map[DEBUG_DUMP]

            @block.sync
            def _(sy):
                waits(sy, sSt, sV, sS, sT)
                sdma(sy, AP(dout.tensor, 0, [[1, 2048 * 1024]]),
                     AP(src_t.tensor, 0, [[1, 2048 * 1024]]))
                waits(sy, sD, sSt)
            return nc

        emit_output(x1)

        @block.sync
        def _(sy):
            waits(sy, sSt)

    return nc


_NC_CACHE = {}


def _get_nc():
    if "nc" not in _NC_CACHE:
        nc = bass.Bass("TRN2", target_bir_lowering=False)
        _NC_CACHE["nc"] = build(nc)
    return _NC_CACHE["nc"]


def kernel(L0_attn_x, L0_attn_ls_scale, L0_attn_rs_scale, L0_attn_split_scale,
           L0_attn_Wk, L0_attn_Wugg, L0_attn_Wout,
           L0_ffn_w_gate, L0_ffn_w_up, L0_ffn_w_down,
           L1_attn_Wq, L1_attn_Wk, L1_attn_Wv, L1_attn_Wo, L1_attn_bo,
           L1_attn_rel_pos_bias,
           L1_ffn_w_gate, L1_ffn_w_up, L1_ffn_w_down):
    x = np.ascontiguousarray(np.asarray(L0_attn_x, dtype=np.float32))
    B = x.shape[0]

    rpb = np.ascontiguousarray(np.asarray(L1_attn_rel_pos_bias, dtype=np.float32))
    p = np.arange(128)[:, None]
    m = np.arange(SCW)[None, :]
    dgrid = m - p - 511
    idx = np.clip(1023 - dgrid, 0, 1023)
    sc = rpb[:, idx]
    sc[:, dgrid < 0] = -1e9
    sc = np.ascontiguousarray(sc, dtype=np.float32)

    common = {
        "L0_ls": np.ascontiguousarray(L0_attn_ls_scale, dtype=np.float32),
        "L0_rs": np.ascontiguousarray(L0_attn_rs_scale, dtype=np.float32),
        "L0_ss": np.ascontiguousarray(L0_attn_split_scale, dtype=np.float32),
        "L0_Wk": np.ascontiguousarray(L0_attn_Wk, dtype=np.float32),
        "L0_Wugg": np.ascontiguousarray(L0_attn_Wugg, dtype=np.float32),
        "L0_Wout": np.ascontiguousarray(L0_attn_Wout, dtype=np.float32),
        "L0_wg": np.ascontiguousarray(L0_ffn_w_gate, dtype=np.float32),
        "L0_wu": np.ascontiguousarray(L0_ffn_w_up, dtype=np.float32),
        "L0_wd": np.ascontiguousarray(L0_ffn_w_down, dtype=np.float32),
        "L1_Wq": np.ascontiguousarray(L1_attn_Wq, dtype=np.float32),
        "L1_Wk": np.ascontiguousarray(L1_attn_Wk, dtype=np.float32),
        "L1_Wv": np.ascontiguousarray(L1_attn_Wv, dtype=np.float32),
        "L1_Wo": np.ascontiguousarray(L1_attn_Wo, dtype=np.float32),
        "L1_bo": np.ascontiguousarray(L1_attn_bo, dtype=np.float32),
        "L1_sc": sc,
        "L1_wg": np.ascontiguousarray(L1_ffn_w_gate, dtype=np.float32),
        "L1_wu": np.ascontiguousarray(L1_ffn_w_up, dtype=np.float32),
        "L1_wd": np.ascontiguousarray(L1_ffn_w_down, dtype=np.float32),
    }
    in_maps = [dict(common, x=np.ascontiguousarray(x[i])) for i in range(B)]
    nc = _get_nc()
    res = run_bass_kernel_spmd(nc, in_maps, core_ids=list(range(B)))
    return np.stack([res.results[i]["out"] for i in range(B)], axis=0)
